# version 08
# baseline (speedup 1.0000x reference)
import numpy as np
import concourse.bass as bass
import concourse.mybir as mybir
from concourse.bass_utils import run_bass_kernel_spmd

F32 = mybir.dt.float32
BF16 = mybir.dt.bfloat16
AF = mybir.ActivationFunctionType
ALU = mybir.AluOpType
AX = mybir.AxisListType.X

NCORES = 8
DEBUG_STOP = None
TRACE = False


class _StopBuild(Exception):
    pass

TCORE = 2048
NBLK = 4
D = 1024
DT = D // 128
NIQ = 16
NKT = 64
EPS = 1e-5
NTOK_BATCH = 8192 * 1024
HD = 4096
NEXP = 8
NJ = 32
TCF = 1024
EGROUPS = [2] * 32

XQ_OFF = 0
XK_OFF = 4096
VV_OFF = 8192
PT_OFF = 12352


def build_program():
    nc = bass.Bass("TRN2", target_bir_lowering=False, num_devices=8)

    xin = nc.dram_tensor("xin", [TCORE, D], F32, kind="ExternalInput").ap()
    wq = nc.dram_tensor("wq", [D, D], F32, kind="ExternalInput").ap()
    wk = nc.dram_tensor("wk", [D, D], F32, kind="ExternalInput").ap()
    wv = nc.dram_tensor("wv", [D, D], F32, kind="ExternalInput").ap()
    rw = nc.dram_tensor("rw", [NEXP, D], F32, kind="ExternalInput").ap()
    w1 = nc.dram_tensor("w1", [NEXP, D, HD], F32, kind="ExternalInput").ap()
    w2 = nc.dram_tensor("w2", [NEXP, HD, D], F32, kind="ExternalInput").ap()
    msel = nc.dram_tensor("msel", [1, 12], F32, kind="ExternalInput").ap()
    out = nc.dram_tensor("out", [TCORE, D], F32, kind="ExternalOutput").ap()

    wts = nc.dram_tensor("wts", [128, 3 * 8192], BF16, kind="Internal").ap()
    ccbuf = nc.dram_tensor("ccbuf", [128, 128], F32, kind="Internal").ap()

    from contextlib import ExitStack
    with ExitStack() as stack:
        ec = stack.enter_context
        BIG = ec(nc.sbuf_tensor("BIG", [128, 16384], F32))
        A4 = ec(nc.sbuf_tensor("A4", [128, 16384], BF16))
        HB = ec(nc.sbuf_tensor("HB", [128, 32768], BF16))
        CBF = ec(nc.sbuf_tensor("CBF", [128, 8192], BF16))
        DBF = ec(nc.sbuf_tensor("DBF", [128, 2048], BF16))
        TOK = ec(nc.sbuf_tensor("TOK", [128, 2048], F32))
        MISC = ec(nc.sbuf_tensor("MISC", [128, 1536], F32))
        CW = ec(nc.sbuf_tensor("CW", [128, 128], F32))
        CCS = ec(nc.sbuf_tensor("CCS", [128, 128], F32))
        IDF = ec(nc.sbuf_tensor("IDF", [128, 128], F32))
        ONE1 = ec(nc.sbuf_tensor("ONE1", [1, 128], BF16))
        ONESF = ec(nc.sbuf_tensor("ONESF", [1, 128], F32))
        ONESC = ec(nc.sbuf_tensor("ONESC", [128, 1], BF16))
        INDS0 = ec(nc.sbuf_tensor("INDS0", [128, 65], BF16))
        INDS64 = ec(nc.sbuf_tensor("INDS64", [128, 65], BF16))
        IDDUP = ec(nc.sbuf_tensor("IDDUP", [128, 128], BF16))
        INDQ = ec(nc.sbuf_tensor("INDQ", [128, 128], BF16))
        RWT = ec(nc.sbuf_tensor("RWT", [128, 64], BF16))
        SELB = ec(nc.sbuf_tensor("SELB", [1, 12], F32))
        SMALLF = ec(nc.sbuf_tensor("SMALLF", [128, 40], F32))
        EPSC = ec(nc.sbuf_tensor("EPSC", [128, 1], F32))
        PA = ec(nc.psum_tensor("PA", [128, 1024], F32))
        PB = ec(nc.psum_tensor("PB", [128, 1024], F32))
        PAV = ec(nc.psum_tensor("PAV", [128, 1024], F32))
        PC = ec(nc.psum_tensor("PC", [128, 512], F32))
        PDS = ec(nc.psum_tensor("PDS", [128, 512], F32))
        dx_h = ec(nc.semaphore("dx"))
        dg_h = ec(nc.semaphore("dg"))
        g1_h = ec(nc.semaphore("g1"))
        pe_h = ec(nc.semaphore("pe"))
        ss_h = ec(nc.semaphore("ss"))
        se_h = ec(nc.semaphore("se"))
        sav_h = ec(nc.semaphore("sav"))
        sv_h = ec(nc.semaphore("sv"))
        sc1_h = ec(nc.semaphore("sc1"))
        scc_h = ec(nc.semaphore("scc"))
        block = ec(nc.Block())
        sems = {"dx": dx_h, "dg": dg_h, "g1": g1_h, "pe": pe_h, "ss": ss_h,
                "se": se_h, "sav": sav_h, "sv": sv_h, "sc1": sc1_h, "scc": scc_h}
        cnt = {k: 0 for k in sems}
        PLAN = {"sync": [], "gpsimd": [], "tensor": [], "vector": [], "scalar": []}

        trunc = [False]

        def em(engine, fn, waits=(), inc=None, incval=None):
            if trunc[0]:
                return
            if inc is not None:
                if incval is None:
                    incval = 16 if inc in ("dx", "dg") else 1
                cnt[inc] += incval

            def closure(eng, fn=fn, waits=tuple(waits), inc=inc, incval=incval):
                for s, v in waits:
                    if v > 0:
                        eng.wait_ge(sems[s], v)
                instr = fn(eng)
                if inc is not None:
                    instr.then_inc(sems[inc], incval)
            PLAN[engine].append(closure)

        def a4(c0, c1, p0=0, p1=128):
            return A4[p0:p1, c0:c1]

        PSLOT = [PA[:, 0:512], PA[:, 512:1024], PAV[:, 0:512],
                 PB[:, 0:512], PB[:, 512:1024], PAV[:, 512:1024]]
        slot_sv = [0] * 6

        def ckpt(name, conv=None):
            if DEBUG_STOP != name:
                return
            allw = [(s, cnt[s]) for s in cnt]
            if conv == "a4":
                for i in range(8):
                    em("vector", lambda e, i=i: e.tensor_copy(
                        BIG[:, i * 2048:(i + 1) * 2048], A4[:, i * 2048:(i + 1) * 2048]),
                       waits=allw if i == 0 else [], inc="sv")
            elif conv == "small":
                em("vector", lambda e: e.tensor_copy(BIG[:, 0:40], SMALLF[:]),
                   waits=allw, inc="sv")
                em("vector", lambda e: e.tensor_copy(BIG[:, 128:256], CCS[:]), inc="sv")
                em("vector", lambda e: e.tensor_copy(BIG[:, 256:384], CW[:]), inc="sv")
            elif conv == "hb":
                for i in range(8):
                    em("vector", lambda e, i=i: e.tensor_copy(
                        BIG[:, i * 2048:(i + 1) * 2048], HB[:, i * 2048:(i + 1) * 2048]),
                       waits=allw if i == 0 else [], inc="sv")
            elif conv == "cbf":
                for i in range(4):
                    em("vector", lambda e, i=i: e.tensor_copy(
                        BIG[:, i * 2048:(i + 1) * 2048], CBF[:, i * 2048:(i + 1) * 2048]),
                       waits=allw if i == 0 else [], inc="sv")
            allw2 = [(s, cnt[s]) for s in cnt]
            em("sync", lambda e: e.dma_start(
                out=out.rearrange("(a p) c -> p a c", p=128),
                in_=BIG[:].rearrange("p (a c) -> p a c", c=1024)),
               waits=allw2, inc="dx")
            trunc[0] = True

        em("gpsimd", lambda e: e.memset(EPSC[:], EPS))
        em("gpsimd", lambda e: e.memset(IDF[:], 0.0))
        em("gpsimd", lambda e: e.affine_select(out=IDF[:], in_=IDF[:],
            compare_op=ALU.not_equal, fill=1.0, base=0,
            pattern=[[-1, 128]], channel_multiplier=1))
        em("gpsimd", lambda e: e.memset(ONE1[:], 1.0))
        em("gpsimd", lambda e: e.memset(ONESF[:], 1.0))
        em("gpsimd", lambda e: e.memset(ONESC[:], 1.0))
        em("gpsimd", lambda e: e.memset(INDS0[:], 0.0))
        em("gpsimd", lambda e: e.memset(INDS0[:, 0:1], 1.0))
        em("gpsimd", lambda e: e.memset(INDS64[:], 0.0))
        em("gpsimd", lambda e: e.memset(INDS64[:, 64:65], 1.0))
        em("gpsimd", lambda e: e.memset(IDDUP[:], 0.0))
        for rh in (0, 64):
            for ch in (0, 64):
                em("gpsimd", lambda e, rh=rh, ch=ch: e.affine_select(
                    out=IDDUP[rh:rh + 64, ch:ch + 64], in_=IDDUP[rh:rh + 64, ch:ch + 64],
                    compare_op=ALU.not_equal, fill=1.0, base=0,
                    pattern=[[-1, 64]], channel_multiplier=1))
        em("gpsimd", lambda e: e.memset(INDQ[:], 0.0))
        for k in range(8):
            em("gpsimd", lambda e, k=k: e.memset(INDQ[0:64, k * 16 + 2 * k: k * 16 + 2 * k + 1], 1.0))
            em("gpsimd", lambda e, k=k: e.memset(INDQ[64:128, k * 16 + 2 * k + 1: k * 16 + 2 * k + 2], 1.0),
               inc="g1" if k == 7 else None)
        g1_done = cnt["g1"]
        em("sync", lambda e: e.dma_start(out=SELB[:], in_=msel), inc="dx")

        wmats = [(wq, 0), (wk, 1), (wv, 2)]
        prev_mat_pe = 0
        store_dx = [0, 0, 0]
        for mat, mi in wmats:
            slot = (mi % 2) * 8192
            em("sync", lambda e, mat=mat: e.dma_start(
                out=BIG[:, 0:8192].rearrange("p (k c) -> p k c", c=1024),
                in_=mat.rearrange("(k p) c -> p k c", p=128)),
               waits=[("pe", prev_mat_pe)], inc="dx")
            ld = cnt["dx"]
            gidx = 0
            evac_sv_base = cnt["sv"]
            hb_war_dx = store_dx[mi - 2] if mi >= 2 else 0
            for kd in range(8):
                for mg in range(2):
                    r = gidx % 6
                    for q in range(4):
                        m = mg * 4 + q
                        em("tensor", lambda e, kd=kd, m=m, r=r, q=q: e.transpose(
                            PSLOT[r][:, q * 128:(q + 1) * 128],
                            BIG[:, m * 1024 + kd * 128: m * 1024 + kd * 128 + 128],
                            IDF[:]),
                           waits=([("dx", ld), ("g1", g1_done),
                                   ("sv", evac_sv_base + gidx - 6 + 1 if gidx >= 6 else 0)]
                                  if q == 0 else []),
                           inc="pe" if q == 3 else None)
                    pv = cnt["pe"]
                    em("vector", lambda e, kd=kd, mg=mg, r=r, slot=slot: e.tensor_copy(
                        HB[:, slot + kd * 1024 + mg * 512: slot + kd * 1024 + mg * 512 + 512],
                        PSLOT[r][:]),
                       waits=([("pe", pv), ("dx", hb_war_dx)] if gidx == 0 else [("pe", pv)]),
                       inc="sv")
                    slot_sv[r] = cnt["sv"]
                    gidx += 1
            prev_mat_pe = cnt["pe"]
            em("sync", lambda e, mi=mi, slot=slot: e.dma_start(
                out=wts[:, mi * 8192:(mi + 1) * 8192], in_=HB[:, slot:slot + 8192]),
               waits=[("sv", cnt["sv"])], inc="dx")
            store_dx[mi] = cnt["dx"]
        prev_store_dx = store_dx[2]
        em("sync", lambda e: e.dma_start(out=BIG[0:8, 0:1024], in_=rw),
           waits=[("pe", prev_mat_pe)], inc="dx")
        rwld = cnt["dx"]
        rw_sv0 = cnt["sv"]
        for k in range(8):
            wl = []
            if k == 0:
                wl = [("dx", rwld), ("sv", rw_sv0)]
            elif k >= 4:
                wl = [("sv", rw_sv0 + k - 4 + 1)]
            em("tensor", lambda e, k=k: e.transpose(
                PSLOT[k % 4][0:128, 0:8], BIG[0:8, k * 128:(k + 1) * 128], IDF[0:8, 0:8]),
               waits=wl, inc="pe")
            pv = cnt["pe"]
            em("vector", lambda e, k=k: e.tensor_copy(RWT[:, k * 8:(k + 1) * 8], PSLOT[k % 4][0:128, 0:8]),
               waits=[("pe", pv)], inc="sv")
            slot_sv[k % 4] = cnt["sv"]
        p0_sv = cnt["sv"]
        p0_pe = cnt["pe"]
        ckpt("p0", conv="hb")

        stats_first = [True]
        pending_ep = [None]
        pav_free_sv = {0: 0, 1: 0}
        qq_evac_sv = [0]
        ln1cols = [(16, 24), (24, 32)]

        for B in range(NBLK):
            blk_t0 = B * 512
            for tt in range(4):
                slot = tt % 2
                em("sync", lambda e, B=B, tt=tt, slot=slot: e.dma_start(
                    out=TOK[:, slot * 1024:(slot + 1) * 1024],
                    in_=xin[B * 512 + tt * 128: B * 512 + tt * 128 + 128, :]),
                   waits=[("pe", cnt["pe"])], inc="dx")
                ldv = cnt["dx"]
                c0, _ = ln1cols[slot]
                s_sum, s_sq, s_s2, s_var, s_r, s_b = range(c0, c0 + 6)
                tok = TOK[:, slot * 1024:(slot + 1) * 1024]
                em("vector", lambda e, tok=tok, s_sum=s_sum: e.scalar_tensor_tensor(
                    DBF[:, 1024:2048], tok, 1.0, tok, ALU.mult, ALU.max,
                    accum_out=SMALLF[:, s_sum:s_sum + 1]),
                   waits=[("dx", ldv)])
                em("vector", lambda e, tok=tok, s_sq=s_sq: e.scalar_tensor_tensor(
                    DBF[:, 1024:2048], tok, 1.0, tok, ALU.mult, ALU.mult,
                    accum_out=SMALLF[:, s_sq:s_sq + 1]),
                   inc="sv")
                f1 = cnt["sv"]
                em("vector", lambda e, a=s_s2, b=s_sum: e.tensor_mul(
                    SMALLF[:, a:a + 1], SMALLF[:, b:b + 1], SMALLF[:, b:b + 1]),
                   waits=[("sv", f1)], inc="sv")
                f2 = cnt["sv"]
                em("vector", lambda e, a=s_var, s2=s_s2, sq=s_sq: e.scalar_tensor_tensor(
                    SMALLF[:, a:a + 1], SMALLF[:, s2:s2 + 1], -1.0 / D,
                    SMALLF[:, sq:sq + 1], ALU.mult, ALU.add),
                   waits=[("sv", f2)], inc="sv")
                vv0 = cnt["sv"]
                em("scalar", lambda e, a=s_var: e.activation(
                    SMALLF[:, a:a + 1], SMALLF[:, a:a + 1], AF.Ln, bias=EPSC[:, 0:1], scale=1.0 / D),
                   waits=[("sv", vv0)])
                em("scalar", lambda e, a=s_r, b=s_var: e.activation(
                    SMALLF[:, a:a + 1], SMALLF[:, b:b + 1], AF.Exp, bias=0.0, scale=-0.5),
                   inc="sc1")
                sc0 = cnt["sc1"]
                em("vector", lambda e, a=s_b, b=s_sum, c=s_r: e.scalar_tensor_tensor(
                    SMALLF[:, a:a + 1], SMALLF[:, b:b + 1], -1.0 / D,
                    SMALLF[:, c:c + 1], ALU.mult, ALU.mult),
                   waits=[("sc1", sc0)], inc="sv")
                f3 = cnt["sv"]
                em("vector", lambda e, tok=tok, r=s_r, b=s_b: e.tensor_scalar(
                    tok, tok, SMALLF[:, r:r + 1], SMALLF[:, b:b + 1], ALU.mult, ALU.add),
                   waits=[("sv", f3)], inc="sv")
                apv = cnt["sv"]
                for g in range(2):
                    r = (tt * 2 + g) % 6
                    for q in range(4):
                        kd = g * 4 + q
                        em("tensor", lambda e, tok=tok, kd=kd, r=r, q=q: e.transpose(
                            PSLOT[r][:, q * 128:(q + 1) * 128],
                            tok[:, kd * 128: kd * 128 + 128], IDF[:]),
                           waits=[("sv", apv), ("sv", slot_sv[r])] if q == 0 else [],
                           inc="pe" if q == 3 else None)
                    pv = cnt["pe"]
                    kd0 = g * 4
                    c0 = B * 512 + tt * 128
                    em("vector", lambda e, kd0=kd0, c0=c0, r=r: e.tensor_copy(
                        BIG[:].rearrange("p (a b) -> p a b", b=2048)[:, kd0:kd0 + 4, c0:c0 + 128],
                        PSLOT[r][:].rearrange("p (a b) -> p a b", b=128)),
                       waits=[("pe", pv)])
                    em("vector", lambda e, kd0=kd0, tt=tt, r=r: e.tensor_copy(
                        CBF[:, 0:4096].rearrange("p (a b) -> p a b", b=512)[:, kd0:kd0 + 4, tt * 128:tt * 128 + 128],
                        PSLOT[r][:].rearrange("p (a b) -> p a b", b=128)),
                       inc="sv")
                    slot_sv[r] = cnt["sv"]

            if DEBUG_STOP == "ln1_%d" % B:
                em("vector", lambda e: e.tensor_copy(BIG[:, 1900:1940], SMALLF[:]),
                   waits=[(s, cnt[s]) for s in cnt], inc="sv")
                em("vector", lambda e: e.tensor_copy(BIG[:, 1940:1941], EPSC[:]), inc="sv")
            ckpt("ln1_%d" % B)
            xn1_sv = cnt["sv"]

            em("vector", lambda e: e.memset(A4[:, VV_OFF:VV_OFF + 4160], 1.0),
               waits=[("sav", cnt["sav"])], inc="sv")
            vvms = cnt["sv"]

            for mat_i, (src_off, dst_name) in enumerate([(0, "q"), (1, "k"), (2, "v")]):
                wslot = ((B * 3 + mat_i) % 2) * 8192
                em("sync", lambda e, src_off=src_off, wslot=wslot: e.dma_start(
                    out=HB[:, wslot:wslot + 8192],
                    in_=wts[:, src_off * 8192:(src_off + 1) * 8192]),
                   waits=[("pe", cnt["pe"]), ("dx", prev_store_dx)], inc="dx")
                wld = cnt["dx"]
                if mat_i < 2:
                    xoff = XQ_OFF if mat_i == 0 else XK_OFF
                    for m in range(8):
                        r = m % 6
                        for kd in range(8):
                            em("tensor", lambda e, m=m, kd=kd, r=r, wslot=wslot: e.matmul(
                                PSLOT[r][:],
                                HB[:, wslot + kd * 1024 + m * 128: wslot + kd * 1024 + m * 128 + 128],
                                CBF[:, kd * 512: kd * 512 + 512],
                                start=(kd == 0), stop=(kd == 7)),
                               waits=([("dx", wld), ("sv", xn1_sv),
                                       ("sv", slot_sv[r])] if kd == 0 else []),
                               inc="pe" if kd == 7 else None)
                        pv = cnt["pe"]
                        em("vector", lambda e, m=m, r=r, xoff=xoff: e.tensor_copy(
                            A4[:, xoff + m * 512: xoff + m * 512 + 512], PSLOT[r][:]),
                           waits=[("pe", pv)], inc="sv")
                        slot_sv[r] = cnt["sv"]
                else:
                    for tch in range(4):
                        for half in range(2):
                            r = (tch * 2 + half) % 6
                            for kd in range(8):
                                em("tensor", lambda e, tch=tch, half=half, kd=kd, r=r, wslot=wslot: e.matmul(
                                    PSLOT[r][:],
                                    CBF[:, kd * 512 + tch * 128: kd * 512 + tch * 128 + 128],
                                    HB[:, wslot + kd * 1024 + half * 512: wslot + kd * 1024 + half * 512 + 512],
                                    start=(kd == 0), stop=(kd == 7)),
                                   waits=([("dx", wld), ("sv", vvms), ("sv", slot_sv[r])] if kd == 0 else []),
                                   inc="pe" if kd == 7 else None)
                            pv = cnt["pe"]
                            em("vector", lambda e, tch=tch, half=half, r=r: e.tensor_copy(
                                A4[:, VV_OFF + tch * 1040 + half * 520: VV_OFF + tch * 1040 + half * 520 + 520]
                                .rearrange("p (i c) -> p i c", c=65)[:, :, 0:64],
                                PSLOT[r][:].rearrange("p (i c) -> p i c", c=64)),
                               waits=[("pe", pv)], inc="sv")
                            slot_sv[r] = cnt["sv"]
            ckpt("proj_%d" % B, conv="a4")
            proj_sv = cnt["sv"]
            proj_pe = cnt["pe"]

            prev_exp = cnt["se"]
            for side, xoff, rowc in ((0, XQ_OFF, 1024), (1, XK_OFF, 1056)):
                for half in range(2):
                    em("vector", lambda e, xoff=xoff, half=half: e.tensor_mul(
                        CBF[:, 4608:6656],
                        A4[:, xoff + half * 2048: xoff + half * 2048 + 2048],
                        A4[:, xoff + half * 2048: xoff + half * 2048 + 2048]),
                       waits=[("sv", proj_sv), ("pe", cnt["pe"])], inc="sv")
                    sqv = cnt["sv"]
                    for kd4 in range(4):
                        kd = half * 4 + kd4
                        em("tensor", lambda e, kd=kd, kd4=kd4, half=half, side=side: e.matmul(
                            PC[0:16, 0:512], INDQ[:, kd * 16: kd * 16 + 16],
                            CBF[:, 4608 + kd4 * 512: 4608 + kd4 * 512 + 512],
                            start=(half == 0 and kd4 == 0), stop=(half == 1 and kd4 == 3)),
                           waits=[("sv", sqv)],
                           inc="pe" if (half == 1 and kd4 == 3) else None)
                pv = cnt["pe"]
                em("vector", lambda e, side=side: e.reduce_max(
                    MISC[0:16, 1100 + side:1101 + side], PC[0:16, 0:512], axis=AX),
                   waits=[("pe", pv)], inc="sv")
                mxv = cnt["sv"]
                em("tensor", lambda e, side=side: e.transpose(
                    PC[0:1, 0:16], MISC[0:16, 1100 + side:1101 + side], IDF[0:16, 0:16]),
                   waits=[("sv", mxv)], inc="pe")
                pv = cnt["pe"]
                em("vector", lambda e, rowc=rowc: e.tensor_copy(MISC[0:1, rowc:rowc + 16], PC[0:1, 0:16]),
                   waits=[("pe", pv)], inc="sv")
            em("vector", lambda e: e.reduce_max(MISC[0:1, 1072:1073], MISC[0:1, 1056:1072], axis=AX),
               inc="sv")
            nv = cnt["sv"]
            em("scalar", lambda e: e.activation(MISC[0:1, 1024:1040], MISC[0:1, 1024:1040], AF.Ln, bias=0.0, scale=1.0),
               waits=[("sv", nv)])
            em("scalar", lambda e: e.activation(MISC[0:1, 1072:1073], MISC[0:1, 1072:1073], AF.Ln, bias=0.0, scale=1.0),
               inc="sc1")
            scv = cnt["sc1"]
            em("vector", lambda e: e.tensor_scalar(
                MISC[0:1, 1024:1040], MISC[0:1, 1024:1040], MISC[0:1, 1072:1073], None, ALU.add),
               waits=[("sc1", scv), ("sv", cnt["sv"])], inc="sv")
            nv = cnt["sv"]
            em("scalar", lambda e: e.activation(MISC[0:1, 1024:1040], MISC[0:1, 1024:1040], AF.Exp, bias=0.0, scale=0.5),
               inc="sc1")
            scv = cnt["sc1"]
            em("vector", lambda e: e.tensor_scalar_mul(MISC[0:1, 1024:1040], MISC[0:1, 1024:1040], -1.0),
               waits=[("sc1", scv)], inc="sv")
            nv = cnt["sv"]
            em("tensor", lambda e: e.matmul(PC[0:128, 0:16], ONESF[0:1, :], MISC[0:1, 1024:1040],
                                            start=True, stop=True),
               waits=[("sv", nv)], inc="pe")
            pv = cnt["pe"]
            em("vector", lambda e: e.tensor_copy(SMALLF[:, 0:16], PC[0:128, 0:16]),
               waits=[("pe", pv), ("se", prev_exp)], inc="sv")
            negu_sv = cnt["sv"]

            pav_free_sv[0] = max(pav_free_sv[0], cnt["sv"])
            pav_free_sv[1] = max(pav_free_sv[1], cnt["sv"])
            qq_evac_sv[0] = max(qq_evac_sv[0], negu_sv)
            for iq in range(NIQ):
                qsrc0 = XQ_OFF + (iq // 2) * 512
                pbase = (iq % 2) * 64
                em("tensor", lambda e, qsrc0=qsrc0, pbase=pbase: e.matmul(
                    PC[0:128, 0:512], IDDUP[pbase:pbase + 64, 0:128],
                    A4[pbase:pbase + 64, qsrc0:qsrc0 + 512],
                    start=True, stop=True, tile_position=(pbase, 0),
                    skip_group_check=True),
                   waits=[("sv", qq_evac_sv[0])], inc="pe")
                qqpe = cnt["pe"]
                em("vector", lambda e: e.tensor_copy(DBF[:, 0:512], PC[0:128, 0:512]),
                   waits=[("pe", qqpe), ("ss", cnt["ss"])], inc="sv")
                qqv = cnt["sv"]
                qq_evac_sv[0] = qqv

                ng = len(EGROUPS)
                sbase = cnt["ss"]
                ebase = cnt["se"]
                for g, glen in enumerate(EGROUPS):
                    ps = PA if g % 2 == 0 else PB
                    em("scalar", lambda e, g=g, glen=glen, ps=ps, iq=iq: e.activation(
                        A4[:, PT_OFF + (g % 2) * 1024: PT_OFF + (g % 2) * 1024 + glen * 512],
                        ps[:, 0:glen * 512], AF.Exp,
                        bias=SMALLF[:, iq:iq + 1], scale=1.0),
                       waits=[("ss", sbase + g + 1), ("sv", negu_sv)],
                       inc="se")
                goff = [0]
                for g2, gl in enumerate(EGROUPS):
                    goff.append(goff[-1] + gl)

                def emit_sgroup(g, glen, ebase=ebase, qqv=qqv):
                    ps = PA if g % 2 == 0 else PB
                    for q in range(glen):
                        t = goff[g] + q
                        rk, ik = t // 16, t % 16
                        h = (ik % 2) * 64
                        wl = []
                        if q == 0:
                            if g >= 2:
                                wl.append(("se", ebase + g - 1))
                            if g == 0:
                                wl.append(("sv", qqv))
                        em("tensor", lambda e, rk=rk, ik=ik, h=h, ps=ps, q=q: e.matmul(
                            ps[:, q * 512:(q + 1) * 512],
                            A4[h:h + 64, XK_OFF + (ik // 2) * 512 + rk * 128:
                               XK_OFF + (ik // 2) * 512 + rk * 128 + 128],
                            DBF[h:h + 64, 0:512],
                            start=True, stop=True, tile_position=(h, 0)),
                           waits=wl, inc="ss" if q == glen - 1 else None)

                def emit_avgroup(g, glen, ebase=ebase, iq=iq):
                    avsl = (iq % 2) * 512
                    for q in range(glen):
                        t = goff[g] + q
                        rk, ik = t // 16, t % 16
                        wl = []
                        if q == 0:
                            wl.append(("se", ebase + g + 1))
                            if g == 0:
                                wl.append(("sv", pav_free_sv[iq % 2]))
                        em("tensor", lambda e, rk=rk, ik=ik, g=g, q=q, t=t, avsl=avsl: e.matmul(
                            PAV[0:65, avsl:avsl + 512],
                            A4[:, VV_OFF + rk * 1040 + ik * 65: VV_OFF + rk * 1040 + ik * 65 + 65],
                            A4[:, PT_OFF + (g % 2) * 1024 + q * 512: PT_OFF + (g % 2) * 1024 + q * 512 + 512],
                            start=(t == 0), stop=(t == NKT - 1), skip_group_check=True),
                           waits=wl, inc="sav" if t == NKT - 1 else None)

                ng2 = len(EGROUPS)
                for g, glen in enumerate(EGROUPS):
                    emit_sgroup(g, glen)
                    if g == 4 and pending_ep[0] is not None:
                        pending_ep[0]()
                        pending_ep[0] = None
                    if g >= 1:
                        emit_avgroup(g - 1, EGROUPS[g - 1])
                emit_avgroup(ng2 - 1, EGROUPS[ng2 - 1])

                avv = cnt["sav"]
                kd, pb = iq // 2, (iq % 2) * 64
                avsl = (iq % 2) * 512
                em("vector", lambda e, pb=pb, avsl=avsl: e.tensor_copy(
                    MISC[pb:pb + 64, 0:512], PAV[0:64, avsl:avsl + 512]),
                   waits=[("sav", avv)])
                em("vector", lambda e, avsl=avsl: e.tensor_copy(
                    MISC[0:1, 1024:1536], PAV[64:65, avsl:avsl + 512]),
                   inc="sv")
                em("vector", lambda e: e.reciprocal(MISC[0:1, 1024:1536], MISC[0:1, 1024:1536]),
                   waits=[("sv", cnt["sv"])], inc="sv")
                ep2 = cnt["sv"]
                first = stats_first[0]
                stats_first[0] = False
                last_stats = (B == NBLK - 1 and iq == NIQ - 1)

                def late_ep(iq=iq, B=B, kd=kd, pb=pb, avsl=avsl, ep2=ep2,
                            first=first, last_stats=last_stats, blk_t0=blk_t0):
                    bigsl = BIG[pb:pb + 64, kd * 2048 + blk_t0: kd * 2048 + blk_t0 + 512]
                    em("tensor", lambda e, pb=pb, avsl=avsl: e.matmul(
                        PAV[0:64, avsl:avsl + 512], ONESF[0:1, 0:64],
                        MISC[0:1, 1024:1536], start=True, stop=True, skip_group_check=True),
                       waits=[("sv", ep2)], inc="pe")
                    bcp = cnt["pe"]
                    em("vector", lambda e, pb=pb, avsl=avsl: e.tensor_mul(
                        MISC[pb:pb + 64, 512:1024], MISC[pb:pb + 64, 0:512],
                        PAV[0:64, avsl:avsl + 512]),
                       waits=[("pe", bcp)], inc="sv")
                    pav_free_sv[iq % 2] = cnt["sv"]
                    em("vector", lambda e, bigsl=bigsl: e.tensor_add(
                        bigsl, bigsl, MISC[pb:pb + 64, 512:1024]))
                    em("vector", lambda e, bigsl=bigsl, pb=pb: e.tensor_copy(
                        CBF[pb:pb + 64, 4096:4608], bigsl),
                       waits=[("pe", cnt["pe"])])
                    em("vector", lambda e, pb=pb: e.tensor_mul(
                        CBF[pb:pb + 64, 4608:5120],
                        CBF[pb:pb + 64, 4096:4608], CBF[pb:pb + 64, 4096:4608]),
                       inc="sv")
                    yv = cnt["sv"]
                    em("tensor", lambda e, first=first, pb=pb: e.matmul(
                        PDS[0:65, 0:512], INDS0[pb:pb + 64, 0:65], CBF[pb:pb + 64, 4096:4608],
                        start=first, stop=False, skip_group_check=True,
                        tile_position=(pb, 0)),
                       waits=[("sv", yv)])
                    em("tensor", lambda e, pb=pb, last_stats=last_stats: e.matmul(
                        PDS[0:65, 0:512], INDS64[pb:pb + 64, 0:65], CBF[pb:pb + 64, 4608:5120],
                        start=False, stop=last_stats, skip_group_check=True,
                        tile_position=(pb, 0)),
                       inc="pe")

                pending_ep[0] = late_ep
                ckpt("attn_%d_%d" % (B, iq))
            if pending_ep[0] is not None:
                pending_ep[0]()
                pending_ep[0] = None
            slot_sv[2] = cnt["sv"]
            slot_sv[5] = cnt["sv"]
            ckpt("blk_%d" % B)

        last_stats_pe = cnt["pe"]

        def fence():
            return [("sv", cnt["sv"])]

        def emf(fn):
            em("vector", fn, waits=fence(), inc="sv")

        em("vector", lambda e: e.tensor_copy(MISC[0:1, 0:512], PDS[0:1, 0:512]),
           waits=[("pe", last_stats_pe)])
        em("vector", lambda e: e.tensor_copy(MISC[0:1, 512:1024], PDS[64:65, 0:512]), inc="sv")
        emf(lambda e: e.reduce_sum(SMALLF[0:1, 34:35], MISC[0:1, 0:512], axis=AX))
        emf(lambda e: e.reduce_sum(SMALLF[0:1, 35:36], MISC[0:1, 512:1024], axis=AX))
        emf(lambda e: e.memset(CCS[:], 0.0))
        for i, sc in enumerate([34, 35, 34, 35]):
            emf(lambda e, i=i, sc=sc: e.tensor_copy(CCS[0:1, i:i + 1], SMALLF[0:1, sc:sc + 1]))
        emf(lambda e: e.tensor_mul(CCS[0:1, 0:4], CCS[0:1, 0:4], SELB[0:1, 0:4]))
        ccv = cnt["sv"]
        em("gpsimd", lambda e: e.dma_start(out=ccbuf, in_=CCS[:]),
           waits=[("sv", ccv)], inc="dg")
        gv = cnt["dg"]
        em("gpsimd", lambda e: e.collective_compute(
            "AllReduce", ALU.add, replica_groups=[[0, 1, 2, 3, 4, 5, 6, 7]],
            ins=[ccbuf], outs=[ccbuf]),
           waits=[("dg", gv)], inc="scc", incval=1)
        em("gpsimd", lambda e: e.dma_start(out=CCS[:], in_=ccbuf),
           waits=[("scc", 1)], inc="dg")
        gv = cnt["dg"]
        em("vector", lambda e: e.tensor_mul(MISC[0:1, 0:4], CCS[0:1, 0:4], SELB[0:1, 4:8]),
           waits=[("dg", gv)], inc="sv")
        emf(lambda e: e.reduce_sum(SMALLF[0:1, 34:35], MISC[0:1, 0:4], axis=AX))
        emf(lambda e: e.tensor_mul(MISC[0:1, 8:12], CCS[0:1, 0:4], SELB[0:1, 8:12]))
        emf(lambda e: e.reduce_sum(SMALLF[0:1, 35:36], MISC[0:1, 8:12], axis=AX))
        emf(lambda e: e.tensor_scalar_mul(SMALLF[0:1, 34:35], SMALLF[0:1, 34:35], 1.0 / NTOK_BATCH))
        emf(lambda e: e.tensor_scalar_mul(SMALLF[0:1, 35:36], SMALLF[0:1, 35:36], 1.0 / NTOK_BATCH))
        emf(lambda e: e.tensor_mul(SMALLF[0:1, 36:37], SMALLF[0:1, 34:35], SMALLF[0:1, 34:35]))
        emf(lambda e: e.tensor_sub(SMALLF[0:1, 35:36], SMALLF[0:1, 35:36], SMALLF[0:1, 36:37]))
        vv0 = cnt["sv"]
        em("scalar", lambda e: e.activation(SMALLF[0:1, 35:36], SMALLF[0:1, 35:36], AF.Ln, bias=EPSC[0:1, 0:1], scale=1.0),
           waits=[("sv", vv0)])
        em("scalar", lambda e: e.activation(SMALLF[0:1, 35:36], SMALLF[0:1, 35:36], AF.Exp, bias=0.0, scale=-0.5),
           inc="sc1")
        scv = cnt["sc1"]
        em("vector", lambda e: e.tensor_mul(SMALLF[0:1, 36:37], SMALLF[0:1, 34:35], SMALLF[0:1, 35:36]),
           waits=[("sc1", scv)], inc="sv")
        emf(lambda e: e.tensor_scalar_mul(SMALLF[0:1, 36:37], SMALLF[0:1, 36:37], -1.0))
        vv0 = cnt["sv"]
        em("tensor", lambda e: e.matmul(PC[0:128, 0:1], ONESF[0:1, :], SMALLF[0:1, 35:36],
                                        start=True, stop=True),
           waits=[("sv", vv0)])
        em("tensor", lambda e: e.matmul(PC[0:128, 1:2], ONESF[0:1, :], SMALLF[0:1, 36:37],
                                        start=True, stop=True), inc="pe")
        pv = cnt["pe"]
        em("vector", lambda e: e.tensor_copy(SMALLF[:, 32:34], PC[0:128, 0:2]),
           waits=[("pe", pv)], inc="sv")
        p2_sv = cnt["sv"]
        ckpt("p2", conv="small")

        for tt in range(16):
            slot = tt % 2
            em("sync", lambda e, tt=tt, slot=slot: e.dma_start(
                out=TOK[:, slot * 1024:(slot + 1) * 1024],
                in_=xin[tt * 128: tt * 128 + 128, :]),
               waits=[("pe", cnt["pe"])], inc="dx")
            ldv = cnt["dx"]
            for g in range(2):
                r = (tt * 2 + g) % 6
                for q in range(4):
                    kd = g * 4 + q
                    em("tensor", lambda e, slot=slot, kd=kd, r=r, q=q: e.transpose(
                        PSLOT[r][:, q * 128:(q + 1) * 128],
                        TOK[:, slot * 1024 + kd * 128: slot * 1024 + kd * 128 + 128], IDF[:]),
                       waits=[("dx", ldv), ("sv", p2_sv), ("sv", slot_sv[r])] if q == 0 else [],
                       inc="pe" if q == 3 else None)
                pv = cnt["pe"]
                kd0 = g * 4
                bigv = BIG[:].rearrange("p (a b) -> p a b", b=2048)[:, kd0:kd0 + 4,
                                                                   tt * 128:tt * 128 + 128]
                em("vector", lambda e, bigv=bigv: e.tensor_scalar(
                    MISC[:, 512:1024].rearrange("p (a b) -> p a b", b=128),
                    bigv, SMALLF[:, 32:33], SMALLF[:, 33:34], ALU.mult, ALU.add),
                   waits=[("pe", pv)])
                em("vector", lambda e, bigv=bigv, r=r: e.tensor_add(
                    bigv, MISC[:, 512:1024].rearrange("p (a b) -> p a b", b=128),
                    PSLOT[r][:].rearrange("p (a b) -> p a b", b=128)),
                   inc="sv")
                slot_sv[r] = cnt["sv"]
        p3_sv = cnt["sv"]
        ckpt("p3")

        for tc in range(4):
            for kd in range(8):
                em("vector", lambda e, kd=kd, tc=tc: e.tensor_copy(
                    CBF[:, 4096:4608], BIG[:, kd * 2048 + tc * 512: kd * 2048 + tc * 512 + 512]),
                   waits=[("sv", p3_sv), ("pe", cnt["pe"])])
                em("vector", lambda e: e.tensor_mul(CBF[:, 4608:5120], CBF[:, 4096:4608], CBF[:, 4096:4608]),
                   inc="sv")
                yv = cnt["sv"]
                em("tensor", lambda e, kd=kd: e.matmul(
                    PA[0:1, 0:512], ONESC[:], CBF[:, 4096:4608],
                    start=(kd == 0), stop=(kd == 7), skip_group_check=True),
                   waits=[("sv", yv)])
                em("tensor", lambda e, kd=kd: e.matmul(
                    PA[0:1, 512:1024], ONESC[:], CBF[:, 4608:5120],
                    start=(kd == 0), stop=(kd == 7), skip_group_check=True),
                   inc="pe")
            pv = cnt["pe"]
            em("vector", lambda e: e.tensor_scalar_mul(TOK[0:1, 0:512], PA[0:1, 0:512], 1.0 / D),
               waits=[("pe", pv)])
            em("vector", lambda e: e.tensor_scalar_mul(TOK[0:1, 512:1024], PA[0:1, 512:1024], 1.0 / D))
            em("vector", lambda e: e.tensor_mul(TOK[0:1, 1024:1536], TOK[0:1, 0:512], TOK[0:1, 0:512]))
            em("vector", lambda e: e.tensor_sub(TOK[0:1, 512:1024], TOK[0:1, 512:1024], TOK[0:1, 1024:1536]),
               inc="sv")
            vv0 = cnt["sv"]
            em("scalar", lambda e: e.activation(TOK[0:1, 512:1024], TOK[0:1, 512:1024], AF.Ln, bias=EPSC[0:1, 0:1], scale=1.0),
               waits=[("sv", vv0)])
            em("scalar", lambda e: e.activation(TOK[0:1, 1536:2048], TOK[0:1, 512:1024], AF.Exp, bias=0.0, scale=-0.5),
               inc="sc1")
            scv = cnt["sc1"]
            em("vector", lambda e: e.tensor_mul(TOK[0:1, 1024:1536], TOK[0:1, 0:512], TOK[0:1, 1536:2048]),
               waits=[("sc1", scv)])
            em("vector", lambda e: e.tensor_scalar_mul(TOK[0:1, 1024:1536], TOK[0:1, 1024:1536], -1.0),
               inc="sv")
            vv0 = cnt["sv"]
            em("tensor", lambda e: e.matmul(PB[0:128, 0:512], ONESF[0:1, :], TOK[0:1, 1536:2048],
                                            start=True, stop=True),
               waits=[("sv", vv0)])
            em("tensor", lambda e: e.matmul(PB[0:128, 512:1024], ONESF[0:1, :], TOK[0:1, 1024:1536],
                                            start=True, stop=True), inc="pe")
            pv = cnt["pe"]
            em("vector", lambda e: e.tensor_copy(MISC[:, 0:512], PB[:, 0:512]),
               waits=[("pe", pv)])
            em("vector", lambda e: e.tensor_copy(MISC[:, 512:1024], PB[:, 512:1024]), inc="sv")
            bcv = cnt["sv"]
            for kd in range(8):
                em("vector", lambda e, kd=kd, tc=tc: e.tensor_mul(
                    TOK[:, 0:512], BIG[:, kd * 2048 + tc * 512: kd * 2048 + tc * 512 + 512],
                    MISC[:, 0:512]),
                   waits=[("sv", bcv)] if kd == 0 else [])
                em("vector", lambda e, kd=kd, tc=tc: e.tensor_add(
                    A4[:, kd * 2048 + tc * 512: kd * 2048 + tc * 512 + 512],
                    TOK[:, 0:512], MISC[:, 512:1024]),
                   inc="sv" if kd == 7 else None)
        p4_sv = cnt["sv"]
        ckpt("p4", conv="a4")

        for tch in range(16):
            for kd in range(8):
                em("tensor", lambda e, tch=tch, kd=kd: e.matmul(
                    PDS[0:128, tch * 8: tch * 8 + 8],
                    A4[:, kd * 2048 + tch * 128: kd * 2048 + tch * 128 + 128],
                    RWT[:, kd * 8: kd * 8 + 8],
                    start=(kd == 0), stop=(kd == 7), skip_group_check=True),
                   waits=[("sv", p4_sv)] if (tch == 0 and kd == 0) else [],
                   inc="pe" if (tch == 15 and kd == 7) else None)
        pv = cnt["pe"]
        em("vector", lambda e: e.tensor_copy(CW[:], PDS[0:128, 0:128]),
           waits=[("pe", pv)], inc="sv")
        cw3 = CW[:].rearrange("p (t e) -> p t e", e=8)
        emf(lambda e: e.reduce_max(TOK[:, 0:16], cw3, axis=AX))
        emf(lambda e: e.tensor_tensor(
            cw3, cw3, TOK[:, 0:16].rearrange("p (t o) -> p t o", o=1).broadcast_to([128, 16, 8]),
            ALU.subtract))
        vv0 = cnt["sv"]
        em("scalar", lambda e: e.activation(CW[:], CW[:], AF.Exp, bias=0.0, scale=1.0),
           waits=[("sv", vv0)], inc="sc1")
        scv = cnt["sc1"]
        em("vector", lambda e: e.reduce_sum(TOK[:, 16:32], cw3, axis=AX),
           waits=[("sc1", scv)], inc="sv")
        emf(lambda e: e.reciprocal(TOK[:, 32:48], TOK[:, 16:32]))
        emf(lambda e: e.tensor_tensor(
            cw3, cw3, TOK[:, 32:48].rearrange("p (t o) -> p t o", o=1).broadcast_to([128, 16, 8]),
            ALU.mult))
        def cwslice(e8):
            return CW[:].rearrange("p (t e) -> p t e", e=8)[:, :, e8:e8 + 1]
        for ei in range(8):
            rsl = TOK[:, 512 + ei * 16: 512 + ei * 16 + 16]
            firstj = True
            for j in range(8):
                if j == ei:
                    continue
                op = ALU.is_ge if j < ei else ALU.is_gt
                if firstj:
                    emf(lambda e, rsl=rsl, j=j, ei=ei, op=op: e.tensor_tensor(
                        rsl.rearrange("p (t o) -> p t o", o=1), cwslice(j), cwslice(ei), op))
                    firstj = False
                else:
                    emf(lambda e, j=j, ei=ei, op=op: e.tensor_tensor(
                        TOK[:, 256:272].rearrange("p (t o) -> p t o", o=1), cwslice(j), cwslice(ei), op))
                    emf(lambda e, rsl=rsl: e.tensor_add(rsl, rsl, TOK[:, 256:272]))
        emf(lambda e: e.memset(TOK[:, 272:288], 0.0))
        for ei in range(8):
            rsl = TOK[:, 512 + ei * 16: 512 + ei * 16 + 16]
            emf(lambda e, rsl=rsl: e.tensor_single_scalar(rsl, rsl, 2.0, ALU.is_lt))
            emf(lambda e, rsl=rsl, ei=ei: e.tensor_tensor(
                cwslice(ei), cwslice(ei), rsl.rearrange("p (t o) -> p t o", o=1), ALU.mult))
            emf(lambda e, ei=ei: e.tensor_tensor(
                TOK[:, 272:288].rearrange("p (t o) -> p t o", o=1),
                TOK[:, 272:288].rearrange("p (t o) -> p t o", o=1), cwslice(ei), ALU.add))
        emf(lambda e: e.tensor_scalar_add(TOK[:, 272:288], TOK[:, 272:288], 1e-9))
        emf(lambda e: e.reciprocal(TOK[:, 288:304], TOK[:, 272:288]))
        for ei in range(8):
            emf(lambda e, ei=ei: e.tensor_tensor(
                cwslice(ei), cwslice(ei),
                TOK[:, 288:304].rearrange("p (t o) -> p t o", o=1), ALU.mult))
        p5_sv = cnt["sv"]
        ckpt("p5", conv="small")

        ffn_war_sv = p5_sv
        for e_i in range(NEXP):
            for TC in range(2):
                t0 = TC * TCF
                for w in range(2):
                    dstc = 0 if w == 0 else 1024
                    for i4 in range(4):
                        col = (TC * 8 + w * 4 + i4) * 8 + e_i
                        em("tensor", lambda e, col=col, i4=i4: e.transpose(
                            PDS[0:1, i4 * 128: i4 * 128 + 128], CW[:, col:col + 1], IDF[:]),
                           waits=([("sv", cnt["sv"])] if i4 == 0 else []),
                           inc="pe" if i4 == 3 else None)
                    pv = cnt["pe"]
                    em("vector", lambda e: e.tensor_copy(TOK[0:1, 0:512], PDS[0:1, 0:512]),
                       waits=[("pe", pv)], inc="sv")
                    vv0 = cnt["sv"]
                    em("tensor", lambda e: e.matmul(PDS[0:128, 0:512], ONESF[0:1, :],
                                                    TOK[0:1, 0:512], start=True, stop=True),
                       waits=[("sv", vv0)], inc="pe")
                    pv = cnt["pe"]
                    em("vector", lambda e, dstc=dstc: e.tensor_copy(MISC[:, dstc:dstc + 512], PDS[:, 0:512]),
                       waits=[("pe", pv)], inc="sv")
                cw_sv = cnt["sv"]
                ckpt("cwb_%d_%d" % (e_i, TC))
                gelu_base = cnt["sc1"]
                for j in range(NJ):
                    dslot = (j % 2) * 1024
                    gwait = gelu_base + j - 1 if j >= 2 else gelu_base
                    em("gpsimd", lambda e, e_i=e_i, j=j, dslot=dslot: e.dma_start(
                        out=DBF[:, dslot:dslot + 1024].rearrange("p (k c) -> p k c", c=128),
                        in_=w1[e_i, :, j * 128:(j + 1) * 128].rearrange("(k p) c -> p k c", p=128)),
                       waits=[("sc1", gwait)],
                       inc="dg")
                    wld = cnt["dg"]
                    ps = PA if j % 2 == 0 else PB
                    pswait = gelu_base + j - 1 if j >= 2 else 0
                    for ts2 in range(2):
                        for kd in range(8):
                            wl = []
                            if ts2 == 0 and kd == 0:
                                wl = [("dg", wld), ("sc1", pswait), ("sv", cw_sv)]
                            em("tensor", lambda e, j=j, ts2=ts2, kd=kd, ps=ps, dslot=dslot, t0=t0: e.matmul(
                                ps[:, ts2 * 512: ts2 * 512 + 512],
                                DBF[:, dslot + kd * 128: dslot + kd * 128 + 128],
                                A4[:, kd * 2048 + t0 + ts2 * 512: kd * 2048 + t0 + ts2 * 512 + 512],
                                start=(kd == 0), stop=(kd == 7), skip_group_check=True),
                               waits=wl,
                               inc="pe" if (ts2 == 1 and kd == 7) else None)
                    pv = cnt["pe"]
                    em("scalar", lambda e, j=j, ps=ps: e.activation(
                        HB[:, j * 1024:(j + 1) * 1024], ps[:, 0:1024], AF.Gelu, bias=0.0, scale=1.0),
                       waits=[("pe", pv)], inc="sc1")
                gelu_all = cnt["sc1"]
                ckpt("h_%d_%d" % (e_i, TC), conv="hb")
                y_evac_sv = {}
                y_chain_sv = {}
                for d_i in range(8):
                    cslot = (d_i % 2) * 4096
                    war = y_evac_sv.get(d_i - 2, ffn_war_sv)
                    for quarter in range(4):
                        em("gpsimd", lambda e, e_i=e_i, d_i=d_i, q4=quarter, cslot=cslot: e.dma_start(
                            out=CBF[:, cslot + q4 * 1024: cslot + q4 * 1024 + 1024]
                            .rearrange("p (j c) -> p j c", c=128),
                            in_=w2[e_i, q4 * 1024:(q4 + 1) * 1024, d_i * 128:(d_i + 1) * 128]
                            .rearrange("(j p) c -> p j c", p=128)),
                           waits=[("sv", war)] if quarter == 0 else [],
                           inc="dg")
                    wld = cnt["dg"]
                    for tsub in range(2):
                        ci = d_i * 2 + tsub
                        psy = PAV[:, 0:512] if ci % 2 == 0 else PAV[:, 512:1024]
                        cwar = y_chain_sv.get(ci - 2, war)
                        for j in range(NJ):
                            wl = []
                            if j == 0:
                                wl = [("dg", wld), ("sc1", gelu_all), ("sv", max(war, cwar))]
                            em("tensor", lambda e, d_i=d_i, tsub=tsub, j=j, psy=psy, cslot=cslot: e.matmul(
                                psy, CBF[:, cslot + j * 128: cslot + j * 128 + 128],
                                HB[:, j * 1024 + tsub * 512: j * 1024 + tsub * 512 + 512],
                                start=(j == 0), stop=(j == NJ - 1), skip_group_check=True),
                               waits=wl,
                               inc="pe" if j == NJ - 1 else None)
                        pv = cnt["pe"]
                        wsl = 0 if tsub == 0 else 1024
                        em("vector", lambda e, psy=psy, wsl=wsl: e.tensor_mul(
                            MISC[:, 512:1024], psy, MISC[:, wsl:wsl + 512]),
                           waits=[("pe", pv)])
                        em("vector", lambda e, d_i=d_i, t0=t0, tsub=tsub: e.tensor_add(
                            BIG[:, d_i * 2048 + t0 + tsub * 512: d_i * 2048 + t0 + tsub * 512 + 512],
                            BIG[:, d_i * 2048 + t0 + tsub * 512: d_i * 2048 + t0 + tsub * 512 + 512],
                            MISC[:, 512:1024]),
                           inc="sv")
                        y_chain_sv[ci] = cnt["sv"]
                    y_evac_sv[d_i] = cnt["sv"]
                    ckpt("yd_%d_%d_%d" % (e_i, TC, d_i))
                ffn_war_sv = cnt["sv"]
                ckpt("ffn_%d_%d" % (e_i, TC))

        ckpt("p6")
        ffn_sv = cnt["sv"]
        for tt in range(16):
            slot = tt % 2
            for g in range(2):
                r = (tt * 2 + g) % 6
                for q in range(4):
                    kd = g * 4 + q
                    em("tensor", lambda e, kd=kd, tt=tt, r=r, q=q: e.transpose(
                        PSLOT[r][:, q * 128:(q + 1) * 128],
                        BIG[:, kd * 2048 + tt * 128: kd * 2048 + tt * 128 + 128], IDF[:]),
                       waits=([("sv", ffn_sv), ("dx", cnt["dx"]), ("sv", slot_sv[r])] if q == 0 else []),
                       inc="pe" if q == 3 else None)
                pv = cnt["pe"]
                em("vector", lambda e, slot=slot, g=g, r=r: e.tensor_copy(
                    TOK[:, slot * 1024 + g * 512: slot * 1024 + g * 512 + 512], PSLOT[r][:]),
                   waits=[("pe", pv)], inc="sv")
                slot_sv[r] = cnt["sv"]
            vv0 = cnt["sv"]
            em("sync", lambda e, tt=tt, slot=slot: e.dma_start(
                out=out[tt * 128: tt * 128 + 128, :],
                in_=TOK[:, slot * 1024: slot * 1024 + 1024]),
               waits=[("sv", vv0)], inc="dx")

        final_dx = cnt["dx"]
        final_counts = dict(cnt)

        def fin(eng, v=final_dx):
            eng.wait_ge(sems["dx"], v)
        PLAN["sync"].append(fin)

        @block.sync
        def _(eng):
            for f in PLAN["sync"]:
                f(eng)

        @block.gpsimd
        def _(eng):
            for f in PLAN["gpsimd"]:
                f(eng)

        @block.tensor
        def _(eng):
            for f in PLAN["tensor"]:
                f(eng)

        @block.vector
        def _(eng):
            for f in PLAN["vector"]:
                f(eng)

        @block.scalar
        def _(eng):
            for f in PLAN["scalar"]:
                f(eng)

    return nc


_NC_CACHE = {}


def kernel(L0_attn_x, L0_attn_Wq, L0_attn_Wk, L0_attn_Wv,
           L0_ffn_router_w, L0_ffn_w1, L0_ffn_w2):
    x = np.ascontiguousarray(np.asarray(L0_attn_x, dtype=np.float32)).reshape(16384, 1024)
    wq = np.ascontiguousarray(np.asarray(L0_attn_Wq, dtype=np.float32))
    wk = np.ascontiguousarray(np.asarray(L0_attn_Wk, dtype=np.float32))
    wv = np.ascontiguousarray(np.asarray(L0_attn_Wv, dtype=np.float32))
    rw = np.ascontiguousarray(np.asarray(L0_ffn_router_w, dtype=np.float32))
    w1 = np.ascontiguousarray(np.asarray(L0_ffn_w1, dtype=np.float32))
    w2 = np.ascontiguousarray(np.asarray(L0_ffn_w2, dtype=np.float32))

    if "nc" not in _NC_CACHE:
        _NC_CACHE["nc"] = build_program()
    nc = _NC_CACHE["nc"]

    in_maps = []
    for c in range(NCORES):
        b = c // 4
        ms = np.zeros((1, 12), dtype=np.float32)
        if b == 0:
            ms[0, 0] = 1.0
            ms[0, 1] = 1.0
            ms[0, 4] = 1.0
            ms[0, 9] = 1.0
        else:
            ms[0, 2] = 1.0
            ms[0, 3] = 1.0
            ms[0, 6] = 1.0
            ms[0, 11] = 1.0
        in_maps.append({
            "xin": x[c * TCORE:(c + 1) * TCORE],
            "wq": wq, "wk": wk, "wv": wv, "rw": rw,
            "w1": w1, "w2": w2, "msel": ms,
        })
    res = run_bass_kernel_spmd(nc, in_maps, core_ids=list(range(NCORES)), trace=TRACE)
    if TRACE and getattr(res, "exec_time_ns", None):
        print("KERNEL exec_time_ns:", res.exec_time_ns)
    full = np.concatenate([np.asarray(res.results[c]["out"]) for c in range(NCORES)], axis=0)
    return full.reshape(2, 8192, 1024).astype(np.float32)


# version 09
# speedup vs baseline: 1.0522x; 1.0522x over previous
import numpy as np
import concourse.bass as bass
import concourse.mybir as mybir
from concourse.bass_utils import run_bass_kernel_spmd

F32 = mybir.dt.float32
BF16 = mybir.dt.bfloat16
AF = mybir.ActivationFunctionType
ALU = mybir.AluOpType
AX = mybir.AxisListType.X

NCORES = 8
DEBUG_STOP = None
TRACE = False


class _StopBuild(Exception):
    pass

TCORE = 2048
NBLK = 4
D = 1024
DT = D // 128
NIQ = 16
NKT = 64
EPS = 1e-5
NTOK_BATCH = 8192 * 1024
HD = 4096
NEXP = 8
NJ = 32
TCF = 1024
EGROUPS = [2] * 32

XQ_OFF = 0
XK_OFF = 4096
VV_OFF = 8192
PT_OFF = 12352


def build_program():
    nc = bass.Bass("TRN2", target_bir_lowering=False, num_devices=8)

    xin = nc.dram_tensor("xin", [TCORE, D], F32, kind="ExternalInput").ap()
    wq = nc.dram_tensor("wq", [D, D], F32, kind="ExternalInput").ap()
    wk = nc.dram_tensor("wk", [D, D], F32, kind="ExternalInput").ap()
    wv = nc.dram_tensor("wv", [D, D], F32, kind="ExternalInput").ap()
    rw = nc.dram_tensor("rw", [NEXP, D], F32, kind="ExternalInput").ap()
    w1 = nc.dram_tensor("w1", [NEXP, D, HD], F32, kind="ExternalInput").ap()
    w2 = nc.dram_tensor("w2", [NEXP, HD, D], F32, kind="ExternalInput").ap()
    msel = nc.dram_tensor("msel", [1, 12], F32, kind="ExternalInput").ap()
    out = nc.dram_tensor("out", [TCORE, D], F32, kind="ExternalOutput").ap()

    wts = nc.dram_tensor("wts", [128, 3 * 8192], BF16, kind="Internal").ap()
    ccbuf = nc.dram_tensor("ccbuf", [128, 128], F32, kind="Internal").ap()

    from contextlib import ExitStack
    with ExitStack() as stack:
        ec = stack.enter_context
        BIG = ec(nc.sbuf_tensor("BIG", [128, 16384], F32))
        A4 = ec(nc.sbuf_tensor("A4", [128, 16384], BF16))
        HB = ec(nc.sbuf_tensor("HB", [128, 32768], BF16))
        CBF = ec(nc.sbuf_tensor("CBF", [128, 8192], BF16))
        DBF = ec(nc.sbuf_tensor("DBF", [128, 2048], BF16))
        TOK = ec(nc.sbuf_tensor("TOK", [128, 2048], F32))
        MISC = ec(nc.sbuf_tensor("MISC", [128, 1536], F32))
        CW = ec(nc.sbuf_tensor("CW", [128, 128], F32))
        CCS = ec(nc.sbuf_tensor("CCS", [128, 128], F32))
        IDF = ec(nc.sbuf_tensor("IDF", [128, 128], F32))
        ONE1 = ec(nc.sbuf_tensor("ONE1", [1, 128], BF16))
        ONESF = ec(nc.sbuf_tensor("ONESF", [1, 128], F32))
        ONESC = ec(nc.sbuf_tensor("ONESC", [128, 1], BF16))
        INDS0 = ec(nc.sbuf_tensor("INDS0", [128, 65], BF16))
        INDS64 = ec(nc.sbuf_tensor("INDS64", [128, 65], BF16))
        IDDUP = ec(nc.sbuf_tensor("IDDUP", [128, 128], BF16))
        INDQ = ec(nc.sbuf_tensor("INDQ", [128, 128], BF16))
        RWT = ec(nc.sbuf_tensor("RWT", [128, 64], BF16))
        SELB = ec(nc.sbuf_tensor("SELB", [1, 12], F32))
        SMALLF = ec(nc.sbuf_tensor("SMALLF", [128, 40], F32))
        EPSC = ec(nc.sbuf_tensor("EPSC", [128, 1], F32))
        PA = ec(nc.psum_tensor("PA", [128, 1024], F32))
        PB = ec(nc.psum_tensor("PB", [128, 1024], F32))
        PAV = ec(nc.psum_tensor("PAV", [128, 1024], F32))
        PC = ec(nc.psum_tensor("PC", [128, 512], F32))
        PDS = ec(nc.psum_tensor("PDS", [128, 512], F32))
        dx_h = ec(nc.semaphore("dx"))
        dg_h = ec(nc.semaphore("dg"))
        g1_h = ec(nc.semaphore("g1"))
        pe_h = ec(nc.semaphore("pe"))
        ss_h = ec(nc.semaphore("ss"))
        se_h = ec(nc.semaphore("se"))
        sav_h = ec(nc.semaphore("sav"))
        sv_h = ec(nc.semaphore("sv"))
        sc1_h = ec(nc.semaphore("sc1"))
        scc_h = ec(nc.semaphore("scc"))
        block = ec(nc.Block())
        sems = {"dx": dx_h, "dg": dg_h, "g1": g1_h, "pe": pe_h, "ss": ss_h,
                "se": se_h, "sav": sav_h, "sv": sv_h, "sc1": sc1_h, "scc": scc_h}
        cnt = {k: 0 for k in sems}
        PLAN = {"sync": [], "gpsimd": [], "tensor": [], "vector": [], "scalar": []}

        trunc = [False]

        def em(engine, fn, waits=(), inc=None, incval=None):
            if trunc[0]:
                return
            if inc is not None:
                if incval is None:
                    incval = 16 if inc in ("dx", "dg") else 1
                cnt[inc] += incval

            def closure(eng, fn=fn, waits=tuple(waits), inc=inc, incval=incval):
                for s, v in waits:
                    if v > 0:
                        eng.wait_ge(sems[s], v)
                instr = fn(eng)
                if inc is not None:
                    instr.then_inc(sems[inc], incval)
            PLAN[engine].append(closure)

        def a4(c0, c1, p0=0, p1=128):
            return A4[p0:p1, c0:c1]

        PSLOT = [PA[:, 0:512], PA[:, 512:1024], PAV[:, 0:512],
                 PB[:, 0:512], PB[:, 512:1024], PAV[:, 512:1024]]
        slot_sv = [0] * 6

        def ckpt(name, conv=None):
            if DEBUG_STOP != name:
                return
            allw = [(s, cnt[s]) for s in cnt]
            if conv == "a4":
                for i in range(8):
                    em("vector", lambda e, i=i: e.tensor_copy(
                        BIG[:, i * 2048:(i + 1) * 2048], A4[:, i * 2048:(i + 1) * 2048]),
                       waits=allw if i == 0 else [], inc="sv")
            elif conv == "small":
                em("vector", lambda e: e.tensor_copy(BIG[:, 0:40], SMALLF[:]),
                   waits=allw, inc="sv")
                em("vector", lambda e: e.tensor_copy(BIG[:, 128:256], CCS[:]), inc="sv")
                em("vector", lambda e: e.tensor_copy(BIG[:, 256:384], CW[:]), inc="sv")
            elif conv == "hb":
                for i in range(8):
                    em("vector", lambda e, i=i: e.tensor_copy(
                        BIG[:, i * 2048:(i + 1) * 2048], HB[:, i * 2048:(i + 1) * 2048]),
                       waits=allw if i == 0 else [], inc="sv")
            elif conv == "cbf":
                for i in range(4):
                    em("vector", lambda e, i=i: e.tensor_copy(
                        BIG[:, i * 2048:(i + 1) * 2048], CBF[:, i * 2048:(i + 1) * 2048]),
                       waits=allw if i == 0 else [], inc="sv")
            allw2 = [(s, cnt[s]) for s in cnt]
            em("sync", lambda e: e.dma_start(
                out=out.rearrange("(a p) c -> p a c", p=128),
                in_=BIG[:].rearrange("p (a c) -> p a c", c=1024)),
               waits=allw2, inc="dx")
            trunc[0] = True

        em("gpsimd", lambda e: e.memset(EPSC[:], EPS))
        em("gpsimd", lambda e: e.memset(IDF[:], 0.0))
        em("gpsimd", lambda e: e.affine_select(out=IDF[:], in_=IDF[:],
            compare_op=ALU.not_equal, fill=1.0, base=0,
            pattern=[[-1, 128]], channel_multiplier=1))
        em("gpsimd", lambda e: e.memset(ONE1[:], 1.0))
        em("gpsimd", lambda e: e.memset(ONESF[:], 1.0))
        em("gpsimd", lambda e: e.memset(ONESC[:], 1.0))
        em("gpsimd", lambda e: e.memset(INDS0[:], 0.0))
        em("gpsimd", lambda e: e.memset(INDS0[:, 0:1], 1.0))
        em("gpsimd", lambda e: e.memset(INDS64[:], 0.0))
        em("gpsimd", lambda e: e.memset(INDS64[:, 64:65], 1.0))
        em("gpsimd", lambda e: e.memset(IDDUP[:], 0.0))
        for rh in (0, 64):
            for ch in (0, 64):
                em("gpsimd", lambda e, rh=rh, ch=ch: e.affine_select(
                    out=IDDUP[rh:rh + 64, ch:ch + 64], in_=IDDUP[rh:rh + 64, ch:ch + 64],
                    compare_op=ALU.not_equal, fill=1.0, base=0,
                    pattern=[[-1, 64]], channel_multiplier=1))
        em("gpsimd", lambda e: e.memset(INDQ[:], 0.0))
        for k in range(8):
            em("gpsimd", lambda e, k=k: e.memset(INDQ[0:64, k * 16 + 2 * k: k * 16 + 2 * k + 1], 1.0))
            em("gpsimd", lambda e, k=k: e.memset(INDQ[64:128, k * 16 + 2 * k + 1: k * 16 + 2 * k + 2], 1.0),
               inc="g1" if k == 7 else None)
        g1_done = cnt["g1"]
        em("sync", lambda e: e.dma_start(out=SELB[:], in_=msel), inc="dx")

        wmats = [(wq, 0), (wk, 1), (wv, 2)]
        prev_mat_pe = 0
        store_dx = [0, 0, 0]
        for mat, mi in wmats:
            slot = (mi % 2) * 8192
            em("sync", lambda e, mat=mat: e.dma_start(
                out=BIG[:, 0:8192].rearrange("p (k c) -> p k c", c=1024),
                in_=mat.rearrange("(k p) c -> p k c", p=128)),
               waits=[("pe", prev_mat_pe)], inc="dx")
            ld = cnt["dx"]
            gidx = 0
            evac_sv_base = cnt["sv"]
            hb_war_dx = store_dx[mi - 2] if mi >= 2 else 0
            for kd in range(8):
                for mg in range(2):
                    r = gidx % 6
                    for q in range(4):
                        m = mg * 4 + q
                        em("tensor", lambda e, kd=kd, m=m, r=r, q=q: e.transpose(
                            PSLOT[r][:, q * 128:(q + 1) * 128],
                            BIG[:, m * 1024 + kd * 128: m * 1024 + kd * 128 + 128],
                            IDF[:]),
                           waits=([("dx", ld), ("g1", g1_done),
                                   ("sv", evac_sv_base + gidx - 6 + 1 if gidx >= 6 else 0)]
                                  if q == 0 else []),
                           inc="pe" if q == 3 else None)
                    pv = cnt["pe"]
                    em("vector", lambda e, kd=kd, mg=mg, r=r, slot=slot: e.tensor_copy(
                        HB[:, slot + kd * 1024 + mg * 512: slot + kd * 1024 + mg * 512 + 512],
                        PSLOT[r][:]),
                       waits=([("pe", pv), ("dx", hb_war_dx)] if gidx == 0 else [("pe", pv)]),
                       inc="sv")
                    slot_sv[r] = cnt["sv"]
                    gidx += 1
            prev_mat_pe = cnt["pe"]
            em("sync", lambda e, mi=mi, slot=slot: e.dma_start(
                out=wts[:, mi * 8192:(mi + 1) * 8192], in_=HB[:, slot:slot + 8192]),
               waits=[("sv", cnt["sv"])], inc="dx")
            store_dx[mi] = cnt["dx"]
        prev_store_dx = store_dx[2]
        em("sync", lambda e: e.dma_start(out=BIG[0:8, 0:1024], in_=rw),
           waits=[("pe", prev_mat_pe)], inc="dx")
        rwld = cnt["dx"]
        rw_sv0 = cnt["sv"]
        for k in range(8):
            wl = []
            if k == 0:
                wl = [("dx", rwld), ("sv", rw_sv0)]
            elif k >= 4:
                wl = [("sv", rw_sv0 + k - 4 + 1)]
            em("tensor", lambda e, k=k: e.transpose(
                PSLOT[k % 4][0:128, 0:8], BIG[0:8, k * 128:(k + 1) * 128], IDF[0:8, 0:8]),
               waits=wl, inc="pe")
            pv = cnt["pe"]
            em("vector", lambda e, k=k: e.tensor_copy(RWT[:, k * 8:(k + 1) * 8], PSLOT[k % 4][0:128, 0:8]),
               waits=[("pe", pv)], inc="sv")
            slot_sv[k % 4] = cnt["sv"]
        p0_sv = cnt["sv"]
        p0_pe = cnt["pe"]
        ckpt("p0", conv="hb")

        stats_first = [True]
        pending_ep = [None]
        pav_free_sv = {0: 0, 1: 0}
        qq_evac_sv = [0]
        ln1cols = [(16, 24), (24, 32)]

        for B in range(NBLK):
            blk_t0 = B * 512
            for tt in range(4):
                slot = tt % 2
                em("sync", lambda e, B=B, tt=tt, slot=slot: e.dma_start(
                    out=TOK[:, slot * 1024:(slot + 1) * 1024],
                    in_=xin[B * 512 + tt * 128: B * 512 + tt * 128 + 128, :]),
                   waits=[("pe", cnt["pe"])], inc="dx")
                ldv = cnt["dx"]
                c0, _ = ln1cols[slot]
                s_sum, s_sq, s_s2, s_var, s_r, s_b = range(c0, c0 + 6)
                tok = TOK[:, slot * 1024:(slot + 1) * 1024]
                em("vector", lambda e, tok=tok, s_sum=s_sum: e.scalar_tensor_tensor(
                    DBF[:, 1024:2048], tok, 1.0, tok, ALU.mult, ALU.max,
                    accum_out=SMALLF[:, s_sum:s_sum + 1]),
                   waits=[("dx", ldv)])
                em("vector", lambda e, tok=tok, s_sq=s_sq: e.scalar_tensor_tensor(
                    DBF[:, 1024:2048], tok, 1.0, tok, ALU.mult, ALU.mult,
                    accum_out=SMALLF[:, s_sq:s_sq + 1]),
                   inc="sv")
                f1 = cnt["sv"]
                em("vector", lambda e, a=s_s2, b=s_sum: e.tensor_mul(
                    SMALLF[:, a:a + 1], SMALLF[:, b:b + 1], SMALLF[:, b:b + 1]),
                   waits=[("sv", f1)], inc="sv")
                f2 = cnt["sv"]
                em("vector", lambda e, a=s_var, s2=s_s2, sq=s_sq: e.scalar_tensor_tensor(
                    SMALLF[:, a:a + 1], SMALLF[:, s2:s2 + 1], -1.0 / D,
                    SMALLF[:, sq:sq + 1], ALU.mult, ALU.add),
                   waits=[("sv", f2)], inc="sv")
                vv0 = cnt["sv"]
                em("scalar", lambda e, a=s_var: e.activation(
                    SMALLF[:, a:a + 1], SMALLF[:, a:a + 1], AF.Ln, bias=EPSC[:, 0:1], scale=1.0 / D),
                   waits=[("sv", vv0)])
                em("scalar", lambda e, a=s_r, b=s_var: e.activation(
                    SMALLF[:, a:a + 1], SMALLF[:, b:b + 1], AF.Exp, bias=0.0, scale=-0.5),
                   inc="sc1")
                sc0 = cnt["sc1"]
                em("vector", lambda e, a=s_b, b=s_sum, c=s_r: e.scalar_tensor_tensor(
                    SMALLF[:, a:a + 1], SMALLF[:, b:b + 1], -1.0 / D,
                    SMALLF[:, c:c + 1], ALU.mult, ALU.mult),
                   waits=[("sc1", sc0)], inc="sv")
                f3 = cnt["sv"]
                em("vector", lambda e, tok=tok, r=s_r, b=s_b: e.tensor_scalar(
                    tok, tok, SMALLF[:, r:r + 1], SMALLF[:, b:b + 1], ALU.mult, ALU.add),
                   waits=[("sv", f3)], inc="sv")
                apv = cnt["sv"]
                for g in range(2):
                    r = (tt * 2 + g) % 6
                    for q in range(4):
                        kd = g * 4 + q
                        em("tensor", lambda e, tok=tok, kd=kd, r=r, q=q: e.transpose(
                            PSLOT[r][:, q * 128:(q + 1) * 128],
                            tok[:, kd * 128: kd * 128 + 128], IDF[:]),
                           waits=[("sv", apv), ("sv", slot_sv[r])] if q == 0 else [],
                           inc="pe" if q == 3 else None)
                    pv = cnt["pe"]
                    kd0 = g * 4
                    c0 = B * 512 + tt * 128
                    em("vector", lambda e, kd0=kd0, c0=c0, r=r: e.tensor_copy(
                        BIG[:].rearrange("p (a b) -> p a b", b=2048)[:, kd0:kd0 + 4, c0:c0 + 128],
                        PSLOT[r][:].rearrange("p (a b) -> p a b", b=128)),
                       waits=[("pe", pv)])
                    em("vector", lambda e, kd0=kd0, tt=tt, r=r: e.tensor_copy(
                        CBF[:, 0:4096].rearrange("p (a b) -> p a b", b=512)[:, kd0:kd0 + 4, tt * 128:tt * 128 + 128],
                        PSLOT[r][:].rearrange("p (a b) -> p a b", b=128)),
                       inc="sv")
                    slot_sv[r] = cnt["sv"]

            if DEBUG_STOP == "ln1_%d" % B:
                em("vector", lambda e: e.tensor_copy(BIG[:, 1900:1940], SMALLF[:]),
                   waits=[(s, cnt[s]) for s in cnt], inc="sv")
                em("vector", lambda e: e.tensor_copy(BIG[:, 1940:1941], EPSC[:]), inc="sv")
            ckpt("ln1_%d" % B)
            xn1_sv = cnt["sv"]

            em("vector", lambda e: e.memset(A4[:, VV_OFF:VV_OFF + 4160], 1.0),
               waits=[("sav", cnt["sav"])], inc="sv")
            vvms = cnt["sv"]

            for mat_i, (src_off, dst_name) in enumerate([(0, "q"), (1, "k"), (2, "v")]):
                wslot = ((B * 3 + mat_i) % 2) * 8192
                em("sync", lambda e, src_off=src_off, wslot=wslot: e.dma_start(
                    out=HB[:, wslot:wslot + 8192],
                    in_=wts[:, src_off * 8192:(src_off + 1) * 8192]),
                   waits=[("pe", cnt["pe"]), ("dx", prev_store_dx)], inc="dx")
                wld = cnt["dx"]
                if mat_i < 2:
                    xoff = XQ_OFF if mat_i == 0 else XK_OFF
                    for m in range(8):
                        r = m % 6
                        for kd in range(8):
                            em("tensor", lambda e, m=m, kd=kd, r=r, wslot=wslot: e.matmul(
                                PSLOT[r][:],
                                HB[:, wslot + kd * 1024 + m * 128: wslot + kd * 1024 + m * 128 + 128],
                                CBF[:, kd * 512: kd * 512 + 512],
                                start=(kd == 0), stop=(kd == 7)),
                               waits=([("dx", wld), ("sv", xn1_sv),
                                       ("sv", slot_sv[r])] if kd == 0 else []),
                               inc="pe" if kd == 7 else None)
                        pv = cnt["pe"]
                        em("vector", lambda e, m=m, r=r, xoff=xoff: e.tensor_copy(
                            A4[:, xoff + m * 512: xoff + m * 512 + 512], PSLOT[r][:]),
                           waits=[("pe", pv)], inc="sv")
                        slot_sv[r] = cnt["sv"]
                else:
                    for tch in range(4):
                        for half in range(2):
                            r = (tch * 2 + half) % 6
                            for kd in range(8):
                                em("tensor", lambda e, tch=tch, half=half, kd=kd, r=r, wslot=wslot: e.matmul(
                                    PSLOT[r][:],
                                    CBF[:, kd * 512 + tch * 128: kd * 512 + tch * 128 + 128],
                                    HB[:, wslot + kd * 1024 + half * 512: wslot + kd * 1024 + half * 512 + 512],
                                    start=(kd == 0), stop=(kd == 7)),
                                   waits=([("dx", wld), ("sv", vvms), ("sv", slot_sv[r])] if kd == 0 else []),
                                   inc="pe" if kd == 7 else None)
                            pv = cnt["pe"]
                            em("vector", lambda e, tch=tch, half=half, r=r: e.tensor_copy(
                                A4[:, VV_OFF + tch * 1040 + half * 520: VV_OFF + tch * 1040 + half * 520 + 520]
                                .rearrange("p (i c) -> p i c", c=65)[:, :, 0:64],
                                PSLOT[r][:].rearrange("p (i c) -> p i c", c=64)),
                               waits=[("pe", pv)], inc="sv")
                            slot_sv[r] = cnt["sv"]
            ckpt("proj_%d" % B, conv="a4")
            proj_sv = cnt["sv"]
            proj_pe = cnt["pe"]

            prev_exp = cnt["se"]
            for side, xoff, rowc in ((0, XQ_OFF, 1024), (1, XK_OFF, 1056)):
                for half in range(2):
                    em("vector", lambda e, xoff=xoff, half=half: e.tensor_mul(
                        CBF[:, 4608:6656],
                        A4[:, xoff + half * 2048: xoff + half * 2048 + 2048],
                        A4[:, xoff + half * 2048: xoff + half * 2048 + 2048]),
                       waits=[("sv", proj_sv), ("pe", cnt["pe"])], inc="sv")
                    sqv = cnt["sv"]
                    for kd4 in range(4):
                        kd = half * 4 + kd4
                        em("tensor", lambda e, kd=kd, kd4=kd4, half=half, side=side: e.matmul(
                            PC[0:16, 0:512], INDQ[:, kd * 16: kd * 16 + 16],
                            CBF[:, 4608 + kd4 * 512: 4608 + kd4 * 512 + 512],
                            start=(half == 0 and kd4 == 0), stop=(half == 1 and kd4 == 3)),
                           waits=[("sv", sqv)],
                           inc="pe" if (half == 1 and kd4 == 3) else None)
                pv = cnt["pe"]
                em("vector", lambda e, side=side: e.reduce_max(
                    MISC[0:16, 1100 + side:1101 + side], PC[0:16, 0:512], axis=AX),
                   waits=[("pe", pv)], inc="sv")
                mxv = cnt["sv"]
                em("tensor", lambda e, side=side: e.transpose(
                    PC[0:1, 0:16], MISC[0:16, 1100 + side:1101 + side], IDF[0:16, 0:16]),
                   waits=[("sv", mxv)], inc="pe")
                pv = cnt["pe"]
                em("vector", lambda e, rowc=rowc: e.tensor_copy(MISC[0:1, rowc:rowc + 16], PC[0:1, 0:16]),
                   waits=[("pe", pv)], inc="sv")
            em("vector", lambda e: e.reduce_max(MISC[0:1, 1072:1073], MISC[0:1, 1056:1072], axis=AX),
               inc="sv")
            nv = cnt["sv"]
            em("scalar", lambda e: e.activation(MISC[0:1, 1024:1040], MISC[0:1, 1024:1040], AF.Ln, bias=0.0, scale=1.0),
               waits=[("sv", nv)])
            em("scalar", lambda e: e.activation(MISC[0:1, 1072:1073], MISC[0:1, 1072:1073], AF.Ln, bias=0.0, scale=1.0),
               inc="sc1")
            scv = cnt["sc1"]
            em("vector", lambda e: e.tensor_scalar(
                MISC[0:1, 1024:1040], MISC[0:1, 1024:1040], MISC[0:1, 1072:1073], None, ALU.add),
               waits=[("sc1", scv), ("sv", cnt["sv"])], inc="sv")
            nv = cnt["sv"]
            em("scalar", lambda e: e.activation(MISC[0:1, 1024:1040], MISC[0:1, 1024:1040], AF.Exp, bias=0.0, scale=0.5),
               inc="sc1")
            scv = cnt["sc1"]
            em("vector", lambda e: e.tensor_scalar_mul(MISC[0:1, 1024:1040], MISC[0:1, 1024:1040], -1.0),
               waits=[("sc1", scv)], inc="sv")
            nv = cnt["sv"]
            em("tensor", lambda e: e.matmul(PC[0:128, 0:16], ONESF[0:1, :], MISC[0:1, 1024:1040],
                                            start=True, stop=True),
               waits=[("sv", nv)], inc="pe")
            pv = cnt["pe"]
            em("vector", lambda e: e.tensor_copy(SMALLF[:, 0:16], PC[0:128, 0:16]),
               waits=[("pe", pv), ("se", prev_exp)], inc="sv")
            negu_sv = cnt["sv"]

            pav_free_sv[0] = max(pav_free_sv[0], cnt["sv"])
            pav_free_sv[1] = max(pav_free_sv[1], cnt["sv"])
            qq_evac_sv[0] = max(qq_evac_sv[0], negu_sv)

            def emit_qq(iq2):
                qsrc0 = XQ_OFF + (iq2 // 2) * 512
                pbase = (iq2 % 2) * 64
                em("tensor", lambda e, qsrc0=qsrc0, pbase=pbase: e.matmul(
                    PC[0:128, 0:512], IDDUP[pbase:pbase + 64, 0:128],
                    A4[pbase:pbase + 64, qsrc0:qsrc0 + 512],
                    start=True, stop=True, tile_position=(pbase, 0),
                    skip_group_check=True),
                   waits=[("sv", qq_evac_sv[0])], inc="pe")
                qqpe = cnt["pe"]
                em("vector", lambda e: e.tensor_copy(DBF[:, 0:512], PC[0:128, 0:512]),
                   waits=[("pe", qqpe), ("ss", cnt["ss"])], inc="sv")
                qq_evac_sv[0] = cnt["sv"]
                return cnt["sv"]

            qq_pending = [emit_qq(0)]
            for iq in range(NIQ):
                qqv = qq_pending[0]

                ng = len(EGROUPS)
                sbase = cnt["ss"]
                ebase = cnt["se"]
                for g, glen in enumerate(EGROUPS):
                    ps = PA if g % 2 == 0 else PB
                    em("scalar", lambda e, g=g, glen=glen, ps=ps, iq=iq: e.activation(
                        A4[:, PT_OFF + (g % 2) * 1024: PT_OFF + (g % 2) * 1024 + glen * 512],
                        ps[:, 0:glen * 512], AF.Exp,
                        bias=SMALLF[:, iq:iq + 1], scale=1.0),
                       waits=[("ss", sbase + g + 1), ("sv", negu_sv)],
                       inc="se")
                goff = [0]
                for g2, gl in enumerate(EGROUPS):
                    goff.append(goff[-1] + gl)

                def emit_sgroup(g, glen, ebase=ebase, qqv=qqv):
                    ps = PA if g % 2 == 0 else PB
                    for q in range(glen):
                        t = goff[g] + q
                        rk, ik = t // 16, t % 16
                        h = (ik % 2) * 64
                        wl = []
                        if q == 0:
                            if g >= 2:
                                wl.append(("se", ebase + g - 1))
                            if g == 0:
                                wl.append(("sv", qqv))
                        em("tensor", lambda e, rk=rk, ik=ik, h=h, ps=ps, q=q: e.matmul(
                            ps[:, q * 512:(q + 1) * 512],
                            A4[h:h + 64, XK_OFF + (ik // 2) * 512 + rk * 128:
                               XK_OFF + (ik // 2) * 512 + rk * 128 + 128],
                            DBF[h:h + 64, 0:512],
                            start=True, stop=True, tile_position=(h, 0)),
                           waits=wl, inc="ss" if q == glen - 1 else None)

                def emit_avgroup(g, glen, ebase=ebase, iq=iq):
                    avsl = (iq % 2) * 512
                    for q in range(glen):
                        t = goff[g] + q
                        rk, ik = t // 16, t % 16
                        wl = []
                        if q == 0:
                            wl.append(("se", ebase + g + 1))
                            if g == 0:
                                wl.append(("sv", pav_free_sv[iq % 2]))
                        em("tensor", lambda e, rk=rk, ik=ik, g=g, q=q, t=t, avsl=avsl: e.matmul(
                            PAV[0:65, avsl:avsl + 512],
                            A4[:, VV_OFF + rk * 1040 + ik * 65: VV_OFF + rk * 1040 + ik * 65 + 65],
                            A4[:, PT_OFF + (g % 2) * 1024 + q * 512: PT_OFF + (g % 2) * 1024 + q * 512 + 512],
                            start=(t == 0), stop=(t == NKT - 1), skip_group_check=True),
                           waits=wl, inc="sav" if t == NKT - 1 else None)

                ng2 = len(EGROUPS)
                for g, glen in enumerate(EGROUPS):
                    emit_sgroup(g, glen)
                    if g == 4 and pending_ep[0] is not None:
                        pending_ep[0]()
                        pending_ep[0] = None
                    if g == ng2 - 1:
                        if iq + 1 < NIQ:
                            qq_pending[0] = emit_qq(iq + 1)
                    if g >= 1:
                        emit_avgroup(g - 1, EGROUPS[g - 1])
                emit_avgroup(ng2 - 1, EGROUPS[ng2 - 1])

                avv = cnt["sav"]
                kd, pb = iq // 2, (iq % 2) * 64
                avsl = (iq % 2) * 512
                em("vector", lambda e, pb=pb, avsl=avsl: e.tensor_copy(
                    MISC[pb:pb + 64, 0:512], PAV[0:64, avsl:avsl + 512]),
                   waits=[("sav", avv)])
                em("vector", lambda e, avsl=avsl: e.tensor_copy(
                    MISC[0:1, 1024:1536], PAV[64:65, avsl:avsl + 512]),
                   inc="sv")
                em("vector", lambda e: e.reciprocal(MISC[0:1, 1024:1536], MISC[0:1, 1024:1536]),
                   waits=[("sv", cnt["sv"])], inc="sv")
                ep2 = cnt["sv"]
                first = stats_first[0]
                stats_first[0] = False
                last_stats = (B == NBLK - 1 and iq == NIQ - 1)

                def late_ep(iq=iq, B=B, kd=kd, pb=pb, avsl=avsl, ep2=ep2,
                            first=first, last_stats=last_stats, blk_t0=blk_t0):
                    bigsl = BIG[pb:pb + 64, kd * 2048 + blk_t0: kd * 2048 + blk_t0 + 512]
                    em("tensor", lambda e, pb=pb, avsl=avsl: e.matmul(
                        PAV[0:64, avsl:avsl + 512], ONESF[0:1, 0:64],
                        MISC[0:1, 1024:1536], start=True, stop=True, skip_group_check=True),
                       waits=[("sv", ep2)], inc="pe")
                    bcp = cnt["pe"]
                    em("vector", lambda e, pb=pb, avsl=avsl: e.tensor_mul(
                        MISC[pb:pb + 64, 512:1024], MISC[pb:pb + 64, 0:512],
                        PAV[0:64, avsl:avsl + 512]),
                       waits=[("pe", bcp)], inc="sv")
                    pav_free_sv[iq % 2] = cnt["sv"]
                    em("vector", lambda e, bigsl=bigsl: e.tensor_add(
                        bigsl, bigsl, MISC[pb:pb + 64, 512:1024]))
                    em("vector", lambda e, bigsl=bigsl, pb=pb: e.tensor_copy(
                        CBF[pb:pb + 64, 4096:4608], bigsl),
                       waits=[("pe", cnt["pe"])])
                    em("vector", lambda e, pb=pb: e.tensor_mul(
                        CBF[pb:pb + 64, 4608:5120],
                        CBF[pb:pb + 64, 4096:4608], CBF[pb:pb + 64, 4096:4608]),
                       inc="sv")
                    yv = cnt["sv"]
                    em("tensor", lambda e, first=first, pb=pb: e.matmul(
                        PDS[0:65, 0:512], INDS0[pb:pb + 64, 0:65], CBF[pb:pb + 64, 4096:4608],
                        start=first, stop=False, skip_group_check=True,
                        tile_position=(pb, 0)),
                       waits=[("sv", yv)])
                    em("tensor", lambda e, pb=pb, last_stats=last_stats: e.matmul(
                        PDS[0:65, 0:512], INDS64[pb:pb + 64, 0:65], CBF[pb:pb + 64, 4608:5120],
                        start=False, stop=last_stats, skip_group_check=True,
                        tile_position=(pb, 0)),
                       inc="pe")

                pending_ep[0] = late_ep
                ckpt("attn_%d_%d" % (B, iq))
            if pending_ep[0] is not None:
                pending_ep[0]()
                pending_ep[0] = None
            slot_sv[2] = cnt["sv"]
            slot_sv[5] = cnt["sv"]
            ckpt("blk_%d" % B)

        last_stats_pe = cnt["pe"]

        def fence():
            return [("sv", cnt["sv"])]

        def emf(fn):
            em("vector", fn, waits=fence(), inc="sv")

        em("vector", lambda e: e.tensor_copy(MISC[0:1, 0:512], PDS[0:1, 0:512]),
           waits=[("pe", last_stats_pe)])
        em("vector", lambda e: e.tensor_copy(MISC[0:1, 512:1024], PDS[64:65, 0:512]), inc="sv")
        emf(lambda e: e.reduce_sum(SMALLF[0:1, 34:35], MISC[0:1, 0:512], axis=AX))
        emf(lambda e: e.reduce_sum(SMALLF[0:1, 35:36], MISC[0:1, 512:1024], axis=AX))
        emf(lambda e: e.memset(CCS[:], 0.0))
        for i, sc in enumerate([34, 35, 34, 35]):
            emf(lambda e, i=i, sc=sc: e.tensor_copy(CCS[0:1, i:i + 1], SMALLF[0:1, sc:sc + 1]))
        emf(lambda e: e.tensor_mul(CCS[0:1, 0:4], CCS[0:1, 0:4], SELB[0:1, 0:4]))
        ccv = cnt["sv"]
        em("gpsimd", lambda e: e.dma_start(out=ccbuf, in_=CCS[:]),
           waits=[("sv", ccv)], inc="dg")
        gv = cnt["dg"]
        em("gpsimd", lambda e: e.collective_compute(
            "AllReduce", ALU.add, replica_groups=[[0, 1, 2, 3, 4, 5, 6, 7]],
            ins=[ccbuf], outs=[ccbuf]),
           waits=[("dg", gv)], inc="scc", incval=1)
        em("gpsimd", lambda e: e.dma_start(out=CCS[:], in_=ccbuf),
           waits=[("scc", 1)], inc="dg")
        gv = cnt["dg"]
        em("vector", lambda e: e.tensor_mul(MISC[0:1, 0:4], CCS[0:1, 0:4], SELB[0:1, 4:8]),
           waits=[("dg", gv)], inc="sv")
        emf(lambda e: e.reduce_sum(SMALLF[0:1, 34:35], MISC[0:1, 0:4], axis=AX))
        emf(lambda e: e.tensor_mul(MISC[0:1, 8:12], CCS[0:1, 0:4], SELB[0:1, 8:12]))
        emf(lambda e: e.reduce_sum(SMALLF[0:1, 35:36], MISC[0:1, 8:12], axis=AX))
        emf(lambda e: e.tensor_scalar_mul(SMALLF[0:1, 34:35], SMALLF[0:1, 34:35], 1.0 / NTOK_BATCH))
        emf(lambda e: e.tensor_scalar_mul(SMALLF[0:1, 35:36], SMALLF[0:1, 35:36], 1.0 / NTOK_BATCH))
        emf(lambda e: e.tensor_mul(SMALLF[0:1, 36:37], SMALLF[0:1, 34:35], SMALLF[0:1, 34:35]))
        emf(lambda e: e.tensor_sub(SMALLF[0:1, 35:36], SMALLF[0:1, 35:36], SMALLF[0:1, 36:37]))
        vv0 = cnt["sv"]
        em("scalar", lambda e: e.activation(SMALLF[0:1, 35:36], SMALLF[0:1, 35:36], AF.Ln, bias=EPSC[0:1, 0:1], scale=1.0),
           waits=[("sv", vv0)])
        em("scalar", lambda e: e.activation(SMALLF[0:1, 35:36], SMALLF[0:1, 35:36], AF.Exp, bias=0.0, scale=-0.5),
           inc="sc1")
        scv = cnt["sc1"]
        em("vector", lambda e: e.tensor_mul(SMALLF[0:1, 36:37], SMALLF[0:1, 34:35], SMALLF[0:1, 35:36]),
           waits=[("sc1", scv)], inc="sv")
        emf(lambda e: e.tensor_scalar_mul(SMALLF[0:1, 36:37], SMALLF[0:1, 36:37], -1.0))
        vv0 = cnt["sv"]
        em("tensor", lambda e: e.matmul(PC[0:128, 0:1], ONESF[0:1, :], SMALLF[0:1, 35:36],
                                        start=True, stop=True),
           waits=[("sv", vv0)])
        em("tensor", lambda e: e.matmul(PC[0:128, 1:2], ONESF[0:1, :], SMALLF[0:1, 36:37],
                                        start=True, stop=True), inc="pe")
        pv = cnt["pe"]
        em("vector", lambda e: e.tensor_copy(SMALLF[:, 32:34], PC[0:128, 0:2]),
           waits=[("pe", pv)], inc="sv")
        p2_sv = cnt["sv"]
        ckpt("p2", conv="small")

        for tt in range(16):
            slot = tt % 2
            em("sync", lambda e, tt=tt, slot=slot: e.dma_start(
                out=TOK[:, slot * 1024:(slot + 1) * 1024],
                in_=xin[tt * 128: tt * 128 + 128, :]),
               waits=[("pe", cnt["pe"])], inc="dx")
            ldv = cnt["dx"]
            for g in range(2):
                r = (tt * 2 + g) % 6
                for q in range(4):
                    kd = g * 4 + q
                    em("tensor", lambda e, slot=slot, kd=kd, r=r, q=q: e.transpose(
                        PSLOT[r][:, q * 128:(q + 1) * 128],
                        TOK[:, slot * 1024 + kd * 128: slot * 1024 + kd * 128 + 128], IDF[:]),
                       waits=[("dx", ldv), ("sv", p2_sv), ("sv", slot_sv[r])] if q == 0 else [],
                       inc="pe" if q == 3 else None)
                pv = cnt["pe"]
                kd0 = g * 4
                bigv = BIG[:].rearrange("p (a b) -> p a b", b=2048)[:, kd0:kd0 + 4,
                                                                   tt * 128:tt * 128 + 128]
                em("vector", lambda e, bigv=bigv: e.tensor_scalar(
                    MISC[:, 512:1024].rearrange("p (a b) -> p a b", b=128),
                    bigv, SMALLF[:, 32:33], SMALLF[:, 33:34], ALU.mult, ALU.add),
                   waits=[("pe", pv)])
                em("vector", lambda e, bigv=bigv, r=r: e.tensor_add(
                    bigv, MISC[:, 512:1024].rearrange("p (a b) -> p a b", b=128),
                    PSLOT[r][:].rearrange("p (a b) -> p a b", b=128)),
                   inc="sv")
                slot_sv[r] = cnt["sv"]
        p3_sv = cnt["sv"]
        ckpt("p3")

        for tc in range(4):
            for kd in range(8):
                em("vector", lambda e, kd=kd, tc=tc: e.tensor_copy(
                    CBF[:, 4096:4608], BIG[:, kd * 2048 + tc * 512: kd * 2048 + tc * 512 + 512]),
                   waits=[("sv", p3_sv), ("pe", cnt["pe"])])
                em("vector", lambda e: e.tensor_mul(CBF[:, 4608:5120], CBF[:, 4096:4608], CBF[:, 4096:4608]),
                   inc="sv")
                yv = cnt["sv"]
                em("tensor", lambda e, kd=kd: e.matmul(
                    PA[0:1, 0:512], ONESC[:], CBF[:, 4096:4608],
                    start=(kd == 0), stop=(kd == 7), skip_group_check=True),
                   waits=[("sv", yv)])
                em("tensor", lambda e, kd=kd: e.matmul(
                    PA[0:1, 512:1024], ONESC[:], CBF[:, 4608:5120],
                    start=(kd == 0), stop=(kd == 7), skip_group_check=True),
                   inc="pe")
            pv = cnt["pe"]
            em("vector", lambda e: e.tensor_scalar_mul(TOK[0:1, 0:512], PA[0:1, 0:512], 1.0 / D),
               waits=[("pe", pv)])
            em("vector", lambda e: e.tensor_scalar_mul(TOK[0:1, 512:1024], PA[0:1, 512:1024], 1.0 / D))
            em("vector", lambda e: e.tensor_mul(TOK[0:1, 1024:1536], TOK[0:1, 0:512], TOK[0:1, 0:512]))
            em("vector", lambda e: e.tensor_sub(TOK[0:1, 512:1024], TOK[0:1, 512:1024], TOK[0:1, 1024:1536]),
               inc="sv")
            vv0 = cnt["sv"]
            em("scalar", lambda e: e.activation(TOK[0:1, 512:1024], TOK[0:1, 512:1024], AF.Ln, bias=EPSC[0:1, 0:1], scale=1.0),
               waits=[("sv", vv0)])
            em("scalar", lambda e: e.activation(TOK[0:1, 1536:2048], TOK[0:1, 512:1024], AF.Exp, bias=0.0, scale=-0.5),
               inc="sc1")
            scv = cnt["sc1"]
            em("vector", lambda e: e.tensor_mul(TOK[0:1, 1024:1536], TOK[0:1, 0:512], TOK[0:1, 1536:2048]),
               waits=[("sc1", scv)])
            em("vector", lambda e: e.tensor_scalar_mul(TOK[0:1, 1024:1536], TOK[0:1, 1024:1536], -1.0),
               inc="sv")
            vv0 = cnt["sv"]
            em("tensor", lambda e: e.matmul(PB[0:128, 0:512], ONESF[0:1, :], TOK[0:1, 1536:2048],
                                            start=True, stop=True),
               waits=[("sv", vv0)])
            em("tensor", lambda e: e.matmul(PB[0:128, 512:1024], ONESF[0:1, :], TOK[0:1, 1024:1536],
                                            start=True, stop=True), inc="pe")
            pv = cnt["pe"]
            em("vector", lambda e: e.tensor_copy(MISC[:, 0:512], PB[:, 0:512]),
               waits=[("pe", pv)])
            em("vector", lambda e: e.tensor_copy(MISC[:, 512:1024], PB[:, 512:1024]), inc="sv")
            bcv = cnt["sv"]
            for kd in range(8):
                em("vector", lambda e, kd=kd, tc=tc: e.tensor_mul(
                    TOK[:, 0:512], BIG[:, kd * 2048 + tc * 512: kd * 2048 + tc * 512 + 512],
                    MISC[:, 0:512]),
                   waits=[("sv", bcv)] if kd == 0 else [])
                em("vector", lambda e, kd=kd, tc=tc: e.tensor_add(
                    A4[:, kd * 2048 + tc * 512: kd * 2048 + tc * 512 + 512],
                    TOK[:, 0:512], MISC[:, 512:1024]),
                   inc="sv" if kd == 7 else None)
        p4_sv = cnt["sv"]
        ckpt("p4", conv="a4")

        for tch in range(16):
            for kd in range(8):
                em("tensor", lambda e, tch=tch, kd=kd: e.matmul(
                    PDS[0:128, tch * 8: tch * 8 + 8],
                    A4[:, kd * 2048 + tch * 128: kd * 2048 + tch * 128 + 128],
                    RWT[:, kd * 8: kd * 8 + 8],
                    start=(kd == 0), stop=(kd == 7), skip_group_check=True),
                   waits=[("sv", p4_sv)] if (tch == 0 and kd == 0) else [],
                   inc="pe" if (tch == 15 and kd == 7) else None)
        pv = cnt["pe"]
        em("vector", lambda e: e.tensor_copy(CW[:], PDS[0:128, 0:128]),
           waits=[("pe", pv)], inc="sv")
        cw3 = CW[:].rearrange("p (t e) -> p t e", e=8)
        emf(lambda e: e.reduce_max(TOK[:, 0:16], cw3, axis=AX))
        emf(lambda e: e.tensor_tensor(
            cw3, cw3, TOK[:, 0:16].rearrange("p (t o) -> p t o", o=1).broadcast_to([128, 16, 8]),
            ALU.subtract))
        vv0 = cnt["sv"]
        em("scalar", lambda e: e.activation(CW[:], CW[:], AF.Exp, bias=0.0, scale=1.0),
           waits=[("sv", vv0)], inc="sc1")
        scv = cnt["sc1"]
        em("vector", lambda e: e.reduce_sum(TOK[:, 16:32], cw3, axis=AX),
           waits=[("sc1", scv)], inc="sv")
        emf(lambda e: e.reciprocal(TOK[:, 32:48], TOK[:, 16:32]))
        emf(lambda e: e.tensor_tensor(
            cw3, cw3, TOK[:, 32:48].rearrange("p (t o) -> p t o", o=1).broadcast_to([128, 16, 8]),
            ALU.mult))
        def cwslice(e8):
            return CW[:].rearrange("p (t e) -> p t e", e=8)[:, :, e8:e8 + 1]
        for ei in range(8):
            rsl = TOK[:, 512 + ei * 16: 512 + ei * 16 + 16]
            firstj = True
            for j in range(8):
                if j == ei:
                    continue
                op = ALU.is_ge if j < ei else ALU.is_gt
                if firstj:
                    emf(lambda e, rsl=rsl, j=j, ei=ei, op=op: e.tensor_tensor(
                        rsl.rearrange("p (t o) -> p t o", o=1), cwslice(j), cwslice(ei), op))
                    firstj = False
                else:
                    emf(lambda e, j=j, ei=ei, op=op: e.tensor_tensor(
                        TOK[:, 256:272].rearrange("p (t o) -> p t o", o=1), cwslice(j), cwslice(ei), op))
                    emf(lambda e, rsl=rsl: e.tensor_add(rsl, rsl, TOK[:, 256:272]))
        emf(lambda e: e.memset(TOK[:, 272:288], 0.0))
        for ei in range(8):
            rsl = TOK[:, 512 + ei * 16: 512 + ei * 16 + 16]
            emf(lambda e, rsl=rsl: e.tensor_single_scalar(rsl, rsl, 2.0, ALU.is_lt))
            emf(lambda e, rsl=rsl, ei=ei: e.tensor_tensor(
                cwslice(ei), cwslice(ei), rsl.rearrange("p (t o) -> p t o", o=1), ALU.mult))
            emf(lambda e, ei=ei: e.tensor_tensor(
                TOK[:, 272:288].rearrange("p (t o) -> p t o", o=1),
                TOK[:, 272:288].rearrange("p (t o) -> p t o", o=1), cwslice(ei), ALU.add))
        emf(lambda e: e.tensor_scalar_add(TOK[:, 272:288], TOK[:, 272:288], 1e-9))
        emf(lambda e: e.reciprocal(TOK[:, 288:304], TOK[:, 272:288]))
        for ei in range(8):
            emf(lambda e, ei=ei: e.tensor_tensor(
                cwslice(ei), cwslice(ei),
                TOK[:, 288:304].rearrange("p (t o) -> p t o", o=1), ALU.mult))
        p5_sv = cnt["sv"]
        ckpt("p5", conv="small")

        ffn_war_sv = p5_sv
        for e_i in range(NEXP):
            for TC in range(2):
                t0 = TC * TCF
                for w in range(2):
                    dstc = 0 if w == 0 else 1024
                    for i4 in range(4):
                        col = (TC * 8 + w * 4 + i4) * 8 + e_i
                        em("tensor", lambda e, col=col, i4=i4: e.transpose(
                            PDS[0:1, i4 * 128: i4 * 128 + 128], CW[:, col:col + 1], IDF[:]),
                           waits=([("sv", cnt["sv"])] if i4 == 0 else []),
                           inc="pe" if i4 == 3 else None)
                    pv = cnt["pe"]
                    em("vector", lambda e: e.tensor_copy(TOK[0:1, 0:512], PDS[0:1, 0:512]),
                       waits=[("pe", pv)], inc="sv")
                    vv0 = cnt["sv"]
                    em("tensor", lambda e: e.matmul(PDS[0:128, 0:512], ONESF[0:1, :],
                                                    TOK[0:1, 0:512], start=True, stop=True),
                       waits=[("sv", vv0)], inc="pe")
                    pv = cnt["pe"]
                    em("vector", lambda e, dstc=dstc: e.tensor_copy(MISC[:, dstc:dstc + 512], PDS[:, 0:512]),
                       waits=[("pe", pv)], inc="sv")
                cw_sv = cnt["sv"]
                ckpt("cwb_%d_%d" % (e_i, TC))
                gelu_base = cnt["sc1"]
                for j in range(NJ):
                    dslot = (j % 2) * 1024
                    gwait = gelu_base + j - 1 if j >= 2 else gelu_base
                    em("gpsimd", lambda e, e_i=e_i, j=j, dslot=dslot: e.dma_start(
                        out=DBF[:, dslot:dslot + 1024].rearrange("p (k c) -> p k c", c=128),
                        in_=w1[e_i, :, j * 128:(j + 1) * 128].rearrange("(k p) c -> p k c", p=128)),
                       waits=[("sc1", gwait)],
                       inc="dg")
                    wld = cnt["dg"]
                    ps = PA if j % 2 == 0 else PB
                    pswait = gelu_base + j - 1 if j >= 2 else 0
                    for ts2 in range(2):
                        for kd in range(8):
                            wl = []
                            if ts2 == 0 and kd == 0:
                                wl = [("dg", wld), ("sc1", pswait), ("sv", cw_sv)]
                            em("tensor", lambda e, j=j, ts2=ts2, kd=kd, ps=ps, dslot=dslot, t0=t0: e.matmul(
                                ps[:, ts2 * 512: ts2 * 512 + 512],
                                DBF[:, dslot + kd * 128: dslot + kd * 128 + 128],
                                A4[:, kd * 2048 + t0 + ts2 * 512: kd * 2048 + t0 + ts2 * 512 + 512],
                                start=(kd == 0), stop=(kd == 7), skip_group_check=True),
                               waits=wl,
                               inc="pe" if (ts2 == 1 and kd == 7) else None)
                    pv = cnt["pe"]
                    em("scalar", lambda e, j=j, ps=ps: e.activation(
                        HB[:, j * 1024:(j + 1) * 1024], ps[:, 0:1024], AF.Gelu, bias=0.0, scale=1.0),
                       waits=[("pe", pv)], inc="sc1")
                gelu_all = cnt["sc1"]
                ckpt("h_%d_%d" % (e_i, TC), conv="hb")
                y_evac_sv = {}
                y_chain_sv = {}
                for d_i in range(8):
                    cslot = (d_i % 2) * 4096
                    war = y_evac_sv.get(d_i - 2, ffn_war_sv)
                    for quarter in range(4):
                        em("gpsimd", lambda e, e_i=e_i, d_i=d_i, q4=quarter, cslot=cslot: e.dma_start(
                            out=CBF[:, cslot + q4 * 1024: cslot + q4 * 1024 + 1024]
                            .rearrange("p (j c) -> p j c", c=128),
                            in_=w2[e_i, q4 * 1024:(q4 + 1) * 1024, d_i * 128:(d_i + 1) * 128]
                            .rearrange("(j p) c -> p j c", p=128)),
                           waits=[("sv", war)] if quarter == 0 else [],
                           inc="dg")
                    wld = cnt["dg"]
                    for tsub in range(2):
                        ci = d_i * 2 + tsub
                        psy = PAV[:, 0:512] if ci % 2 == 0 else PAV[:, 512:1024]
                        cwar = y_chain_sv.get(ci - 2, war)
                        for j in range(NJ):
                            wl = []
                            if j == 0:
                                wl = [("dg", wld), ("sc1", gelu_all), ("sv", max(war, cwar))]
                            em("tensor", lambda e, d_i=d_i, tsub=tsub, j=j, psy=psy, cslot=cslot: e.matmul(
                                psy, CBF[:, cslot + j * 128: cslot + j * 128 + 128],
                                HB[:, j * 1024 + tsub * 512: j * 1024 + tsub * 512 + 512],
                                start=(j == 0), stop=(j == NJ - 1), skip_group_check=True),
                               waits=wl,
                               inc="pe" if j == NJ - 1 else None)
                        pv = cnt["pe"]
                        wsl = 0 if tsub == 0 else 1024
                        em("vector", lambda e, psy=psy, wsl=wsl: e.tensor_mul(
                            MISC[:, 512:1024], psy, MISC[:, wsl:wsl + 512]),
                           waits=[("pe", pv)])
                        em("vector", lambda e, d_i=d_i, t0=t0, tsub=tsub: e.tensor_add(
                            BIG[:, d_i * 2048 + t0 + tsub * 512: d_i * 2048 + t0 + tsub * 512 + 512],
                            BIG[:, d_i * 2048 + t0 + tsub * 512: d_i * 2048 + t0 + tsub * 512 + 512],
                            MISC[:, 512:1024]),
                           inc="sv")
                        y_chain_sv[ci] = cnt["sv"]
                    y_evac_sv[d_i] = cnt["sv"]
                    ckpt("yd_%d_%d_%d" % (e_i, TC, d_i))
                ffn_war_sv = cnt["sv"]
                ckpt("ffn_%d_%d" % (e_i, TC))

        ckpt("p6")
        ffn_sv = cnt["sv"]
        for tt in range(16):
            slot = tt % 2
            for g in range(2):
                r = (tt * 2 + g) % 6
                for q in range(4):
                    kd = g * 4 + q
                    em("tensor", lambda e, kd=kd, tt=tt, r=r, q=q: e.transpose(
                        PSLOT[r][:, q * 128:(q + 1) * 128],
                        BIG[:, kd * 2048 + tt * 128: kd * 2048 + tt * 128 + 128], IDF[:]),
                       waits=([("sv", ffn_sv), ("dx", cnt["dx"]), ("sv", slot_sv[r])] if q == 0 else []),
                       inc="pe" if q == 3 else None)
                pv = cnt["pe"]
                em("vector", lambda e, slot=slot, g=g, r=r: e.tensor_copy(
                    TOK[:, slot * 1024 + g * 512: slot * 1024 + g * 512 + 512], PSLOT[r][:]),
                   waits=[("pe", pv)], inc="sv")
                slot_sv[r] = cnt["sv"]
            vv0 = cnt["sv"]
            em("sync", lambda e, tt=tt, slot=slot: e.dma_start(
                out=out[tt * 128: tt * 128 + 128, :],
                in_=TOK[:, slot * 1024: slot * 1024 + 1024]),
               waits=[("sv", vv0)], inc="dx")

        final_dx = cnt["dx"]
        final_counts = dict(cnt)

        def fin(eng, v=final_dx):
            eng.wait_ge(sems["dx"], v)
        PLAN["sync"].append(fin)

        @block.sync
        def _(eng):
            for f in PLAN["sync"]:
                f(eng)

        @block.gpsimd
        def _(eng):
            for f in PLAN["gpsimd"]:
                f(eng)

        @block.tensor
        def _(eng):
            for f in PLAN["tensor"]:
                f(eng)

        @block.vector
        def _(eng):
            for f in PLAN["vector"]:
                f(eng)

        @block.scalar
        def _(eng):
            for f in PLAN["scalar"]:
                f(eng)

    return nc


_NC_CACHE = {}


def kernel(L0_attn_x, L0_attn_Wq, L0_attn_Wk, L0_attn_Wv,
           L0_ffn_router_w, L0_ffn_w1, L0_ffn_w2):
    x = np.ascontiguousarray(np.asarray(L0_attn_x, dtype=np.float32)).reshape(16384, 1024)
    wq = np.ascontiguousarray(np.asarray(L0_attn_Wq, dtype=np.float32))
    wk = np.ascontiguousarray(np.asarray(L0_attn_Wk, dtype=np.float32))
    wv = np.ascontiguousarray(np.asarray(L0_attn_Wv, dtype=np.float32))
    rw = np.ascontiguousarray(np.asarray(L0_ffn_router_w, dtype=np.float32))
    w1 = np.ascontiguousarray(np.asarray(L0_ffn_w1, dtype=np.float32))
    w2 = np.ascontiguousarray(np.asarray(L0_ffn_w2, dtype=np.float32))

    if "nc" not in _NC_CACHE:
        _NC_CACHE["nc"] = build_program()
    nc = _NC_CACHE["nc"]

    in_maps = []
    for c in range(NCORES):
        b = c // 4
        ms = np.zeros((1, 12), dtype=np.float32)
        if b == 0:
            ms[0, 0] = 1.0
            ms[0, 1] = 1.0
            ms[0, 4] = 1.0
            ms[0, 9] = 1.0
        else:
            ms[0, 2] = 1.0
            ms[0, 3] = 1.0
            ms[0, 6] = 1.0
            ms[0, 11] = 1.0
        in_maps.append({
            "xin": x[c * TCORE:(c + 1) * TCORE],
            "wq": wq, "wk": wk, "wv": wv, "rw": rw,
            "w1": w1, "w2": w2, "msel": ms,
        })
    res = run_bass_kernel_spmd(nc, in_maps, core_ids=list(range(NCORES)), trace=TRACE)
    if TRACE and getattr(res, "exec_time_ns", None):
        print("KERNEL exec_time_ns:", res.exec_time_ns)
    full = np.concatenate([np.asarray(res.results[c]["out"]) for c in range(NCORES)], axis=0)
    return full.reshape(2, 8192, 1024).astype(np.float32)


# version 10 (final)
# speedup vs baseline: 1.1659x; 1.1081x over previous
import numpy as np
import concourse.bass as bass
import concourse.mybir as mybir
from concourse.bass_utils import run_bass_kernel_spmd

F32 = mybir.dt.float32
BF16 = mybir.dt.bfloat16
AF = mybir.ActivationFunctionType
ALU = mybir.AluOpType
AX = mybir.AxisListType.X

NCORES = 8
DEBUG_STOP = None
TRACE = False


class _StopBuild(Exception):
    pass

TCORE = 2048
NBLK = 4
D = 1024
DT = D // 128
NIQ = 16
NKT = 64
EPS = 1e-5
NTOK_BATCH = 8192 * 1024
HD = 4096
NEXP = 8
NJ = 32
TCF = 1024
EGROUPS = [2] * 32

XQ_OFF = 0
XK_OFF = 4096
VV_OFF = 8192
PT_OFF = 12352


def build_program():
    nc = bass.Bass("TRN2", target_bir_lowering=False, num_devices=8)

    xin = nc.dram_tensor("xin", [TCORE, D], F32, kind="ExternalInput").ap()
    wq = nc.dram_tensor("wq", [D, D], F32, kind="ExternalInput").ap()
    wk = nc.dram_tensor("wk", [D, D], F32, kind="ExternalInput").ap()
    wv = nc.dram_tensor("wv", [D, D], F32, kind="ExternalInput").ap()
    rw = nc.dram_tensor("rw", [NEXP, D], F32, kind="ExternalInput").ap()
    w1 = nc.dram_tensor("w1", [NEXP, D, HD], F32, kind="ExternalInput").ap()
    w2 = nc.dram_tensor("w2", [NEXP, HD, D], F32, kind="ExternalInput").ap()
    msel = nc.dram_tensor("msel", [1, 12], F32, kind="ExternalInput").ap()
    out = nc.dram_tensor("out", [TCORE, D], F32, kind="ExternalOutput").ap()

    wts = nc.dram_tensor("wts", [128, 3 * 8192], BF16, kind="Internal").ap()
    ccbuf = nc.dram_tensor("ccbuf", [128, 128], F32, kind="Internal").ap()

    from contextlib import ExitStack
    with ExitStack() as stack:
        ec = stack.enter_context
        BIG = ec(nc.sbuf_tensor("BIG", [128, 16384], F32))
        A4 = ec(nc.sbuf_tensor("A4", [128, 16384], BF16))
        HB = ec(nc.sbuf_tensor("HB", [128, 32768], BF16))
        CBF = ec(nc.sbuf_tensor("CBF", [128, 8192], BF16))
        DBF = ec(nc.sbuf_tensor("DBF", [128, 4096], BF16))
        TOK = ec(nc.sbuf_tensor("TOK", [128, 2048], F32))
        MISC = ec(nc.sbuf_tensor("MISC", [128, 1536], F32))
        CW = ec(nc.sbuf_tensor("CW", [128, 128], F32))
        CCS = ec(nc.sbuf_tensor("CCS", [128, 128], F32))
        IDF = ec(nc.sbuf_tensor("IDF", [128, 128], F32))
        ONE1 = ec(nc.sbuf_tensor("ONE1", [1, 128], BF16))
        ONESF = ec(nc.sbuf_tensor("ONESF", [1, 128], F32))
        ONESC = ec(nc.sbuf_tensor("ONESC", [128, 1], BF16))
        INDS0 = ec(nc.sbuf_tensor("INDS0", [128, 65], BF16))
        INDS64 = ec(nc.sbuf_tensor("INDS64", [128, 65], BF16))
        IDDUP = ec(nc.sbuf_tensor("IDDUP", [128, 128], BF16))
        INDQ = ec(nc.sbuf_tensor("INDQ", [128, 128], BF16))
        RWT = ec(nc.sbuf_tensor("RWT", [128, 64], BF16))
        SELB = ec(nc.sbuf_tensor("SELB", [1, 12], F32))
        SMALLF = ec(nc.sbuf_tensor("SMALLF", [128, 40], F32))
        EPSC = ec(nc.sbuf_tensor("EPSC", [128, 1], F32))
        PA = ec(nc.psum_tensor("PA", [128, 1024], F32))
        PB = ec(nc.psum_tensor("PB", [128, 1024], F32))
        PAV = ec(nc.psum_tensor("PAV", [128, 1024], F32))
        PC = ec(nc.psum_tensor("PC", [128, 512], F32))
        PDS = ec(nc.psum_tensor("PDS", [128, 512], F32))
        dx_h = ec(nc.semaphore("dx"))
        dg_h = ec(nc.semaphore("dg"))
        g1_h = ec(nc.semaphore("g1"))
        pe_h = ec(nc.semaphore("pe"))
        ss_h = ec(nc.semaphore("ss"))
        se_h = ec(nc.semaphore("se"))
        sav_h = ec(nc.semaphore("sav"))
        sv_h = ec(nc.semaphore("sv"))
        sc1_h = ec(nc.semaphore("sc1"))
        scc_h = ec(nc.semaphore("scc"))
        block = ec(nc.Block())
        sems = {"dx": dx_h, "dg": dg_h, "g1": g1_h, "pe": pe_h, "ss": ss_h,
                "se": se_h, "sav": sav_h, "sv": sv_h, "sc1": sc1_h, "scc": scc_h}
        cnt = {k: 0 for k in sems}
        PLAN = {"sync": [], "gpsimd": [], "tensor": [], "vector": [], "scalar": []}

        trunc = [False]

        def em(engine, fn, waits=(), inc=None, incval=None):
            if trunc[0]:
                return
            if inc is not None:
                if incval is None:
                    incval = 16 if inc in ("dx", "dg") else 1
                cnt[inc] += incval

            def closure(eng, fn=fn, waits=tuple(waits), inc=inc, incval=incval):
                for s, v in waits:
                    if v > 0:
                        eng.wait_ge(sems[s], v)
                instr = fn(eng)
                if inc is not None:
                    instr.then_inc(sems[inc], incval)
            PLAN[engine].append(closure)

        def a4(c0, c1, p0=0, p1=128):
            return A4[p0:p1, c0:c1]

        PSLOT = [PA[:, 0:512], PA[:, 512:1024], PAV[:, 0:512],
                 PB[:, 0:512], PB[:, 512:1024], PAV[:, 512:1024]]
        slot_sv = [0] * 6

        def ckpt(name, conv=None):
            if DEBUG_STOP != name:
                return
            allw = [(s, cnt[s]) for s in cnt]
            if conv == "a4":
                for i in range(8):
                    em("vector", lambda e, i=i: e.tensor_copy(
                        BIG[:, i * 2048:(i + 1) * 2048], A4[:, i * 2048:(i + 1) * 2048]),
                       waits=allw if i == 0 else [], inc="sv")
            elif conv == "small":
                em("vector", lambda e: e.tensor_copy(BIG[:, 0:40], SMALLF[:]),
                   waits=allw, inc="sv")
                em("vector", lambda e: e.tensor_copy(BIG[:, 128:256], CCS[:]), inc="sv")
                em("vector", lambda e: e.tensor_copy(BIG[:, 256:384], CW[:]), inc="sv")
            elif conv == "hb":
                for i in range(8):
                    em("vector", lambda e, i=i: e.tensor_copy(
                        BIG[:, i * 2048:(i + 1) * 2048], HB[:, i * 2048:(i + 1) * 2048]),
                       waits=allw if i == 0 else [], inc="sv")
            elif conv == "cbf":
                for i in range(4):
                    em("vector", lambda e, i=i: e.tensor_copy(
                        BIG[:, i * 2048:(i + 1) * 2048], CBF[:, i * 2048:(i + 1) * 2048]),
                       waits=allw if i == 0 else [], inc="sv")
            allw2 = [(s, cnt[s]) for s in cnt]
            em("sync", lambda e: e.dma_start(
                out=out.rearrange("(a p) c -> p a c", p=128),
                in_=BIG[:].rearrange("p (a c) -> p a c", c=1024)),
               waits=allw2, inc="dx")
            trunc[0] = True

        em("gpsimd", lambda e: e.memset(EPSC[:], EPS))
        em("gpsimd", lambda e: e.memset(IDF[:], 0.0))
        em("gpsimd", lambda e: e.affine_select(out=IDF[:], in_=IDF[:],
            compare_op=ALU.not_equal, fill=1.0, base=0,
            pattern=[[-1, 128]], channel_multiplier=1))
        em("gpsimd", lambda e: e.memset(ONE1[:], 1.0))
        em("gpsimd", lambda e: e.memset(ONESF[:], 1.0))
        em("gpsimd", lambda e: e.memset(ONESC[:], 1.0))
        em("gpsimd", lambda e: e.memset(INDS0[:], 0.0))
        em("gpsimd", lambda e: e.memset(INDS0[:, 0:1], 1.0))
        em("gpsimd", lambda e: e.memset(INDS64[:], 0.0))
        em("gpsimd", lambda e: e.memset(INDS64[:, 64:65], 1.0))
        em("gpsimd", lambda e: e.memset(IDDUP[:], 0.0))
        for rh in (0, 64):
            for ch in (0, 64):
                em("gpsimd", lambda e, rh=rh, ch=ch: e.affine_select(
                    out=IDDUP[rh:rh + 64, ch:ch + 64], in_=IDDUP[rh:rh + 64, ch:ch + 64],
                    compare_op=ALU.not_equal, fill=1.0, base=0,
                    pattern=[[-1, 64]], channel_multiplier=1))
        em("gpsimd", lambda e: e.memset(INDQ[:], 0.0))
        for k in range(8):
            em("gpsimd", lambda e, k=k: e.memset(INDQ[0:64, k * 16 + 2 * k: k * 16 + 2 * k + 1], 1.0))
            em("gpsimd", lambda e, k=k: e.memset(INDQ[64:128, k * 16 + 2 * k + 1: k * 16 + 2 * k + 2], 1.0),
               inc="g1" if k == 7 else None)
        g1_done = cnt["g1"]
        em("sync", lambda e: e.dma_start(out=SELB[:], in_=msel), inc="dx")

        wmats = [(wq, 0), (wk, 1), (wv, 2)]
        prev_mat_pe = 0
        store_dx = [0, 0, 0]
        for mat, mi in wmats:
            slot = (mi % 2) * 8192
            em("sync", lambda e, mat=mat: e.dma_start(
                out=BIG[:, 0:8192].rearrange("p (k c) -> p k c", c=1024),
                in_=mat.rearrange("(k p) c -> p k c", p=128)),
               waits=[("pe", prev_mat_pe)], inc="dx")
            ld = cnt["dx"]
            gidx = 0
            evac_sv_base = cnt["sv"]
            hb_war_dx = store_dx[mi - 2] if mi >= 2 else 0
            for kd in range(8):
                for mg in range(2):
                    r = gidx % 6
                    for q in range(4):
                        m = mg * 4 + q
                        em("tensor", lambda e, kd=kd, m=m, r=r, q=q: e.transpose(
                            PSLOT[r][:, q * 128:(q + 1) * 128],
                            BIG[:, m * 1024 + kd * 128: m * 1024 + kd * 128 + 128],
                            IDF[:]),
                           waits=([("dx", ld), ("g1", g1_done),
                                   ("sv", evac_sv_base + gidx - 6 + 1 if gidx >= 6 else 0)]
                                  if q == 0 else []),
                           inc="pe" if q == 3 else None)
                    pv = cnt["pe"]
                    em("vector", lambda e, kd=kd, mg=mg, r=r, slot=slot: e.tensor_copy(
                        HB[:, slot + kd * 1024 + mg * 512: slot + kd * 1024 + mg * 512 + 512],
                        PSLOT[r][:]),
                       waits=([("pe", pv), ("dx", hb_war_dx)] if gidx == 0 else [("pe", pv)]),
                       inc="sv")
                    slot_sv[r] = cnt["sv"]
                    gidx += 1
            prev_mat_pe = cnt["pe"]
            em("sync", lambda e, mi=mi, slot=slot: e.dma_start(
                out=wts[:, mi * 8192:(mi + 1) * 8192], in_=HB[:, slot:slot + 8192]),
               waits=[("sv", cnt["sv"])], inc="dx")
            store_dx[mi] = cnt["dx"]
        prev_store_dx = store_dx[2]
        em("sync", lambda e: e.dma_start(out=BIG[0:8, 0:1024], in_=rw),
           waits=[("pe", prev_mat_pe)], inc="dx")
        rwld = cnt["dx"]
        rw_sv0 = cnt["sv"]
        for k in range(8):
            wl = []
            if k == 0:
                wl = [("dx", rwld), ("sv", rw_sv0)]
            elif k >= 4:
                wl = [("sv", rw_sv0 + k - 4 + 1)]
            em("tensor", lambda e, k=k: e.transpose(
                PSLOT[k % 4][0:128, 0:8], BIG[0:8, k * 128:(k + 1) * 128], IDF[0:8, 0:8]),
               waits=wl, inc="pe")
            pv = cnt["pe"]
            em("vector", lambda e, k=k: e.tensor_copy(RWT[:, k * 8:(k + 1) * 8], PSLOT[k % 4][0:128, 0:8]),
               waits=[("pe", pv)], inc="sv")
            slot_sv[k % 4] = cnt["sv"]
        p0_sv = cnt["sv"]
        p0_pe = cnt["pe"]
        ckpt("p0", conv="hb")

        stats_first = [True]
        pending_ep = [None]
        pav_free_sv = {0: 0, 1: 0}
        qq_evac_sv = [0]
        ln1cols = [(16, 24), (24, 32)]

        for B in range(NBLK):
            blk_t0 = B * 512
            for tt in range(4):
                slot = tt % 2
                em("sync", lambda e, B=B, tt=tt, slot=slot: e.dma_start(
                    out=TOK[:, slot * 1024:(slot + 1) * 1024],
                    in_=xin[B * 512 + tt * 128: B * 512 + tt * 128 + 128, :]),
                   waits=[("pe", cnt["pe"])], inc="dx")
                ldv = cnt["dx"]
                c0, _ = ln1cols[slot]
                s_sum, s_sq, s_s2, s_var, s_r, s_b = range(c0, c0 + 6)
                tok = TOK[:, slot * 1024:(slot + 1) * 1024]
                em("vector", lambda e, tok=tok, s_sum=s_sum: e.scalar_tensor_tensor(
                    DBF[:, 1024:2048], tok, 1.0, tok, ALU.mult, ALU.max,
                    accum_out=SMALLF[:, s_sum:s_sum + 1]),
                   waits=[("dx", ldv)])
                em("vector", lambda e, tok=tok, s_sq=s_sq: e.scalar_tensor_tensor(
                    DBF[:, 1024:2048], tok, 1.0, tok, ALU.mult, ALU.mult,
                    accum_out=SMALLF[:, s_sq:s_sq + 1]),
                   inc="sv")
                f1 = cnt["sv"]
                em("vector", lambda e, a=s_s2, b=s_sum: e.tensor_mul(
                    SMALLF[:, a:a + 1], SMALLF[:, b:b + 1], SMALLF[:, b:b + 1]),
                   waits=[("sv", f1)], inc="sv")
                f2 = cnt["sv"]
                em("vector", lambda e, a=s_var, s2=s_s2, sq=s_sq: e.scalar_tensor_tensor(
                    SMALLF[:, a:a + 1], SMALLF[:, s2:s2 + 1], -1.0 / D,
                    SMALLF[:, sq:sq + 1], ALU.mult, ALU.add),
                   waits=[("sv", f2)], inc="sv")
                vv0 = cnt["sv"]
                em("scalar", lambda e, a=s_var: e.activation(
                    SMALLF[:, a:a + 1], SMALLF[:, a:a + 1], AF.Ln, bias=EPSC[:, 0:1], scale=1.0 / D),
                   waits=[("sv", vv0)])
                em("scalar", lambda e, a=s_r, b=s_var: e.activation(
                    SMALLF[:, a:a + 1], SMALLF[:, b:b + 1], AF.Exp, bias=0.0, scale=-0.5),
                   inc="sc1")
                sc0 = cnt["sc1"]
                em("vector", lambda e, a=s_b, b=s_sum, c=s_r: e.scalar_tensor_tensor(
                    SMALLF[:, a:a + 1], SMALLF[:, b:b + 1], -1.0 / D,
                    SMALLF[:, c:c + 1], ALU.mult, ALU.mult),
                   waits=[("sc1", sc0)], inc="sv")
                f3 = cnt["sv"]
                em("vector", lambda e, tok=tok, r=s_r, b=s_b: e.tensor_scalar(
                    tok, tok, SMALLF[:, r:r + 1], SMALLF[:, b:b + 1], ALU.mult, ALU.add),
                   waits=[("sv", f3)], inc="sv")
                apv = cnt["sv"]
                for g in range(2):
                    r = (tt * 2 + g) % 6
                    for q in range(4):
                        kd = g * 4 + q
                        em("tensor", lambda e, tok=tok, kd=kd, r=r, q=q: e.transpose(
                            PSLOT[r][:, q * 128:(q + 1) * 128],
                            tok[:, kd * 128: kd * 128 + 128], IDF[:]),
                           waits=[("sv", apv), ("sv", slot_sv[r])] if q == 0 else [],
                           inc="pe" if q == 3 else None)
                    pv = cnt["pe"]
                    kd0 = g * 4
                    c0 = B * 512 + tt * 128
                    em("vector", lambda e, kd0=kd0, c0=c0, r=r: e.tensor_copy(
                        BIG[:].rearrange("p (a b) -> p a b", b=2048)[:, kd0:kd0 + 4, c0:c0 + 128],
                        PSLOT[r][:].rearrange("p (a b) -> p a b", b=128)),
                       waits=[("pe", pv)])
                    em("vector", lambda e, kd0=kd0, tt=tt, r=r: e.tensor_copy(
                        CBF[:, 0:4096].rearrange("p (a b) -> p a b", b=512)[:, kd0:kd0 + 4, tt * 128:tt * 128 + 128],
                        PSLOT[r][:].rearrange("p (a b) -> p a b", b=128)),
                       inc="sv")
                    slot_sv[r] = cnt["sv"]

            if DEBUG_STOP == "ln1_%d" % B:
                em("vector", lambda e: e.tensor_copy(BIG[:, 1900:1940], SMALLF[:]),
                   waits=[(s, cnt[s]) for s in cnt], inc="sv")
                em("vector", lambda e: e.tensor_copy(BIG[:, 1940:1941], EPSC[:]), inc="sv")
            ckpt("ln1_%d" % B)
            xn1_sv = cnt["sv"]

            em("vector", lambda e: e.memset(A4[:, VV_OFF:VV_OFF + 4160], 1.0),
               waits=[("sav", cnt["sav"])], inc="sv")
            vvms = cnt["sv"]

            for mat_i, (src_off, dst_name) in enumerate([(0, "q"), (1, "k"), (2, "v")]):
                wslot = ((B * 3 + mat_i) % 2) * 8192
                em("sync", lambda e, src_off=src_off, wslot=wslot: e.dma_start(
                    out=HB[:, wslot:wslot + 8192],
                    in_=wts[:, src_off * 8192:(src_off + 1) * 8192]),
                   waits=[("pe", cnt["pe"]), ("dx", prev_store_dx)], inc="dx")
                wld = cnt["dx"]
                if mat_i < 2:
                    xoff = XQ_OFF if mat_i == 0 else XK_OFF
                    for m in range(8):
                        r = m % 6
                        for kd in range(8):
                            em("tensor", lambda e, m=m, kd=kd, r=r, wslot=wslot: e.matmul(
                                PSLOT[r][:],
                                HB[:, wslot + kd * 1024 + m * 128: wslot + kd * 1024 + m * 128 + 128],
                                CBF[:, kd * 512: kd * 512 + 512],
                                start=(kd == 0), stop=(kd == 7)),
                               waits=([("dx", wld), ("sv", xn1_sv),
                                       ("sv", slot_sv[r])] if kd == 0 else []),
                               inc="pe" if kd == 7 else None)
                        pv = cnt["pe"]
                        em("vector", lambda e, m=m, r=r, xoff=xoff: e.tensor_copy(
                            A4[:, xoff + m * 512: xoff + m * 512 + 512], PSLOT[r][:]),
                           waits=[("pe", pv)], inc="sv")
                        slot_sv[r] = cnt["sv"]
                else:
                    for tch in range(4):
                        for half in range(2):
                            r = (tch * 2 + half) % 6
                            for kd in range(8):
                                em("tensor", lambda e, tch=tch, half=half, kd=kd, r=r, wslot=wslot: e.matmul(
                                    PSLOT[r][:],
                                    CBF[:, kd * 512 + tch * 128: kd * 512 + tch * 128 + 128],
                                    HB[:, wslot + kd * 1024 + half * 512: wslot + kd * 1024 + half * 512 + 512],
                                    start=(kd == 0), stop=(kd == 7)),
                                   waits=([("dx", wld), ("sv", vvms), ("sv", slot_sv[r])] if kd == 0 else []),
                                   inc="pe" if kd == 7 else None)
                            pv = cnt["pe"]
                            em("vector", lambda e, tch=tch, half=half, r=r: e.tensor_copy(
                                A4[:, VV_OFF + tch * 1040 + half * 520: VV_OFF + tch * 1040 + half * 520 + 520]
                                .rearrange("p (i c) -> p i c", c=65)[:, :, 0:64],
                                PSLOT[r][:].rearrange("p (i c) -> p i c", c=64)),
                               waits=[("pe", pv)], inc="sv")
                            slot_sv[r] = cnt["sv"]
            ckpt("proj_%d" % B, conv="a4")
            proj_sv = cnt["sv"]
            proj_pe = cnt["pe"]

            prev_exp = cnt["se"]
            for side, xoff, rowc in ((0, XQ_OFF, 1024), (1, XK_OFF, 1056)):
                for half in range(2):
                    em("vector", lambda e, xoff=xoff, half=half: e.tensor_mul(
                        CBF[:, 4608:6656],
                        A4[:, xoff + half * 2048: xoff + half * 2048 + 2048],
                        A4[:, xoff + half * 2048: xoff + half * 2048 + 2048]),
                       waits=[("sv", proj_sv), ("pe", cnt["pe"])], inc="sv")
                    sqv = cnt["sv"]
                    for kd4 in range(4):
                        kd = half * 4 + kd4
                        em("tensor", lambda e, kd=kd, kd4=kd4, half=half, side=side: e.matmul(
                            PC[0:16, 0:512], INDQ[:, kd * 16: kd * 16 + 16],
                            CBF[:, 4608 + kd4 * 512: 4608 + kd4 * 512 + 512],
                            start=(half == 0 and kd4 == 0), stop=(half == 1 and kd4 == 3)),
                           waits=[("sv", sqv)],
                           inc="pe" if (half == 1 and kd4 == 3) else None)
                pv = cnt["pe"]
                em("vector", lambda e, side=side: e.reduce_max(
                    MISC[0:16, 1100 + side:1101 + side], PC[0:16, 0:512], axis=AX),
                   waits=[("pe", pv)], inc="sv")
                mxv = cnt["sv"]
                em("tensor", lambda e, side=side: e.transpose(
                    PC[0:1, 0:16], MISC[0:16, 1100 + side:1101 + side], IDF[0:16, 0:16]),
                   waits=[("sv", mxv)], inc="pe")
                pv = cnt["pe"]
                em("vector", lambda e, rowc=rowc: e.tensor_copy(MISC[0:1, rowc:rowc + 16], PC[0:1, 0:16]),
                   waits=[("pe", pv)], inc="sv")
            em("vector", lambda e: e.reduce_max(MISC[0:1, 1072:1073], MISC[0:1, 1056:1072], axis=AX),
               inc="sv")
            nv = cnt["sv"]
            em("scalar", lambda e: e.activation(MISC[0:1, 1024:1040], MISC[0:1, 1024:1040], AF.Ln, bias=0.0, scale=1.0),
               waits=[("sv", nv)])
            em("scalar", lambda e: e.activation(MISC[0:1, 1072:1073], MISC[0:1, 1072:1073], AF.Ln, bias=0.0, scale=1.0),
               inc="sc1")
            scv = cnt["sc1"]
            em("vector", lambda e: e.tensor_scalar(
                MISC[0:1, 1024:1040], MISC[0:1, 1024:1040], MISC[0:1, 1072:1073], None, ALU.add),
               waits=[("sc1", scv), ("sv", cnt["sv"])], inc="sv")
            nv = cnt["sv"]
            em("scalar", lambda e: e.activation(MISC[0:1, 1024:1040], MISC[0:1, 1024:1040], AF.Exp, bias=0.0, scale=0.5),
               inc="sc1")
            scv = cnt["sc1"]
            em("vector", lambda e: e.tensor_scalar_mul(MISC[0:1, 1024:1040], MISC[0:1, 1024:1040], -1.0),
               waits=[("sc1", scv)], inc="sv")
            nv = cnt["sv"]
            em("tensor", lambda e: e.matmul(PC[0:128, 0:16], ONESF[0:1, :], MISC[0:1, 1024:1040],
                                            start=True, stop=True),
               waits=[("sv", nv)], inc="pe")
            pv = cnt["pe"]
            em("vector", lambda e: e.tensor_copy(SMALLF[:, 0:16], PC[0:128, 0:16]),
               waits=[("pe", pv), ("se", prev_exp)], inc="sv")
            negu_sv = cnt["sv"]

            pav_free_sv[0] = max(pav_free_sv[0], cnt["sv"])
            pav_free_sv[1] = max(pav_free_sv[1], cnt["sv"])
            qq_evac_sv[0] = max(qq_evac_sv[0], negu_sv)

            def emit_qq(iq2):
                qsrc0 = XQ_OFF + (iq2 // 2) * 512
                pbase = (iq2 % 2) * 64
                em("tensor", lambda e, qsrc0=qsrc0, pbase=pbase: e.matmul(
                    PC[0:128, 0:512], IDDUP[pbase:pbase + 64, 0:128],
                    A4[pbase:pbase + 64, qsrc0:qsrc0 + 512],
                    start=True, stop=True, tile_position=(pbase, 0),
                    skip_group_check=True),
                   waits=[("sv", qq_evac_sv[0])], inc="pe")
                qqpe = cnt["pe"]
                em("vector", lambda e: e.tensor_copy(DBF[:, 0:512], PC[0:128, 0:512]),
                   waits=[("pe", qqpe), ("ss", cnt["ss"])], inc="sv")
                qq_evac_sv[0] = cnt["sv"]
                return cnt["sv"]

            qq_pending = [emit_qq(0)]
            for iq in range(NIQ):
                qqv = qq_pending[0]

                ng = len(EGROUPS)
                sbase = cnt["ss"]
                ebase = cnt["se"]
                for g, glen in enumerate(EGROUPS):
                    ps = PA if g % 2 == 0 else PB
                    em("scalar", lambda e, g=g, glen=glen, ps=ps, iq=iq: e.activation(
                        A4[:, PT_OFF + (g % 2) * 1024: PT_OFF + (g % 2) * 1024 + glen * 512],
                        ps[:, 0:glen * 512], AF.Exp,
                        bias=SMALLF[:, iq:iq + 1], scale=1.0),
                       waits=[("ss", sbase + g + 1), ("sv", negu_sv)],
                       inc="se")
                goff = [0]
                for g2, gl in enumerate(EGROUPS):
                    goff.append(goff[-1] + gl)

                def emit_sgroup(g, glen, ebase=ebase, qqv=qqv):
                    ps = PA if g % 2 == 0 else PB
                    for q in range(glen):
                        t = goff[g] + q
                        rk, ik = t // 16, t % 16
                        h = (ik % 2) * 64
                        wl = []
                        if q == 0:
                            if g >= 2:
                                wl.append(("se", ebase + g - 1))
                            if g == 0:
                                wl.append(("sv", qqv))
                        em("tensor", lambda e, rk=rk, ik=ik, h=h, ps=ps, q=q: e.matmul(
                            ps[:, q * 512:(q + 1) * 512],
                            A4[h:h + 64, XK_OFF + (ik // 2) * 512 + rk * 128:
                               XK_OFF + (ik // 2) * 512 + rk * 128 + 128],
                            DBF[h:h + 64, 0:512],
                            start=True, stop=True, tile_position=(h, 0)),
                           waits=wl, inc="ss" if q == glen - 1 else None)

                def emit_avgroup(g, glen, ebase=ebase, iq=iq):
                    avsl = (iq % 2) * 512
                    for q in range(glen):
                        t = goff[g] + q
                        rk, ik = t // 16, t % 16
                        wl = []
                        if q == 0:
                            wl.append(("se", ebase + g + 1))
                            if g == 0:
                                wl.append(("sv", pav_free_sv[iq % 2]))
                        em("tensor", lambda e, rk=rk, ik=ik, g=g, q=q, t=t, avsl=avsl: e.matmul(
                            PAV[0:65, avsl:avsl + 512],
                            A4[:, VV_OFF + rk * 1040 + ik * 65: VV_OFF + rk * 1040 + ik * 65 + 65],
                            A4[:, PT_OFF + (g % 2) * 1024 + q * 512: PT_OFF + (g % 2) * 1024 + q * 512 + 512],
                            start=(t == 0), stop=(t == NKT - 1), skip_group_check=True),
                           waits=wl, inc="sav" if t == NKT - 1 else None)

                ng2 = len(EGROUPS)
                for g, glen in enumerate(EGROUPS):
                    emit_sgroup(g, glen)
                    if g == 4 and pending_ep[0] is not None:
                        pending_ep[0]()
                        pending_ep[0] = None
                    if g == ng2 - 1:
                        if iq + 1 < NIQ:
                            qq_pending[0] = emit_qq(iq + 1)
                    if g >= 1:
                        emit_avgroup(g - 1, EGROUPS[g - 1])
                emit_avgroup(ng2 - 1, EGROUPS[ng2 - 1])

                avv = cnt["sav"]
                kd, pb = iq // 2, (iq % 2) * 64
                avsl = (iq % 2) * 512
                em("vector", lambda e, pb=pb, avsl=avsl: e.tensor_copy(
                    MISC[pb:pb + 64, 0:512], PAV[0:64, avsl:avsl + 512]),
                   waits=[("sav", avv)])
                em("vector", lambda e, avsl=avsl: e.tensor_copy(
                    MISC[0:1, 1024:1536], PAV[64:65, avsl:avsl + 512]),
                   inc="sv")
                em("vector", lambda e: e.reciprocal(MISC[0:1, 1024:1536], MISC[0:1, 1024:1536]),
                   waits=[("sv", cnt["sv"])], inc="sv")
                ep2 = cnt["sv"]
                first = stats_first[0]
                stats_first[0] = False
                last_stats = (B == NBLK - 1 and iq == NIQ - 1)

                def late_ep(iq=iq, B=B, kd=kd, pb=pb, avsl=avsl, ep2=ep2,
                            first=first, last_stats=last_stats, blk_t0=blk_t0):
                    bigsl = BIG[pb:pb + 64, kd * 2048 + blk_t0: kd * 2048 + blk_t0 + 512]
                    em("tensor", lambda e, pb=pb, avsl=avsl: e.matmul(
                        PAV[0:64, avsl:avsl + 512], ONESF[0:1, 0:64],
                        MISC[0:1, 1024:1536], start=True, stop=True, skip_group_check=True),
                       waits=[("sv", ep2)], inc="pe")
                    bcp = cnt["pe"]
                    em("vector", lambda e, pb=pb, avsl=avsl: e.tensor_mul(
                        MISC[pb:pb + 64, 512:1024], MISC[pb:pb + 64, 0:512],
                        PAV[0:64, avsl:avsl + 512]),
                       waits=[("pe", bcp)], inc="sv")
                    pav_free_sv[iq % 2] = cnt["sv"]
                    em("vector", lambda e, bigsl=bigsl: e.tensor_add(
                        bigsl, bigsl, MISC[pb:pb + 64, 512:1024]))
                    em("vector", lambda e, bigsl=bigsl, pb=pb: e.tensor_copy(
                        CBF[pb:pb + 64, 4096:4608], bigsl),
                       waits=[("pe", cnt["pe"])])
                    em("vector", lambda e, pb=pb: e.tensor_mul(
                        CBF[pb:pb + 64, 4608:5120],
                        CBF[pb:pb + 64, 4096:4608], CBF[pb:pb + 64, 4096:4608]),
                       inc="sv")
                    yv = cnt["sv"]
                    em("tensor", lambda e, first=first, pb=pb: e.matmul(
                        PDS[0:65, 0:512], INDS0[pb:pb + 64, 0:65], CBF[pb:pb + 64, 4096:4608],
                        start=first, stop=False, skip_group_check=True,
                        tile_position=(pb, 0)),
                       waits=[("sv", yv)])
                    em("tensor", lambda e, pb=pb, last_stats=last_stats: e.matmul(
                        PDS[0:65, 0:512], INDS64[pb:pb + 64, 0:65], CBF[pb:pb + 64, 4608:5120],
                        start=False, stop=last_stats, skip_group_check=True,
                        tile_position=(pb, 0)),
                       inc="pe")

                pending_ep[0] = late_ep
                ckpt("attn_%d_%d" % (B, iq))
            if pending_ep[0] is not None:
                pending_ep[0]()
                pending_ep[0] = None
            slot_sv[2] = cnt["sv"]
            slot_sv[5] = cnt["sv"]
            ckpt("blk_%d" % B)

        last_stats_pe = cnt["pe"]

        def fence():
            return [("sv", cnt["sv"])]

        def emf(fn):
            em("vector", fn, waits=fence(), inc="sv")

        em("vector", lambda e: e.tensor_copy(MISC[0:1, 0:512], PDS[0:1, 0:512]),
           waits=[("pe", last_stats_pe)])
        em("vector", lambda e: e.tensor_copy(MISC[0:1, 512:1024], PDS[64:65, 0:512]), inc="sv")
        emf(lambda e: e.reduce_sum(SMALLF[0:1, 34:35], MISC[0:1, 0:512], axis=AX))
        emf(lambda e: e.reduce_sum(SMALLF[0:1, 35:36], MISC[0:1, 512:1024], axis=AX))
        emf(lambda e: e.memset(CCS[:], 0.0))
        for i, sc in enumerate([34, 35, 34, 35]):
            emf(lambda e, i=i, sc=sc: e.tensor_copy(CCS[0:1, i:i + 1], SMALLF[0:1, sc:sc + 1]))
        emf(lambda e: e.tensor_mul(CCS[0:1, 0:4], CCS[0:1, 0:4], SELB[0:1, 0:4]))
        ccv = cnt["sv"]
        em("gpsimd", lambda e: e.dma_start(out=ccbuf, in_=CCS[:]),
           waits=[("sv", ccv)], inc="dg")
        gv = cnt["dg"]
        em("gpsimd", lambda e: e.collective_compute(
            "AllReduce", ALU.add, replica_groups=[[0, 1, 2, 3, 4, 5, 6, 7]],
            ins=[ccbuf], outs=[ccbuf]),
           waits=[("dg", gv)], inc="scc", incval=1)
        em("gpsimd", lambda e: e.dma_start(out=CCS[:], in_=ccbuf),
           waits=[("scc", 1)], inc="dg")
        gv = cnt["dg"]
        em("vector", lambda e: e.tensor_mul(MISC[0:1, 0:4], CCS[0:1, 0:4], SELB[0:1, 4:8]),
           waits=[("dg", gv)], inc="sv")
        emf(lambda e: e.reduce_sum(SMALLF[0:1, 34:35], MISC[0:1, 0:4], axis=AX))
        emf(lambda e: e.tensor_mul(MISC[0:1, 8:12], CCS[0:1, 0:4], SELB[0:1, 8:12]))
        emf(lambda e: e.reduce_sum(SMALLF[0:1, 35:36], MISC[0:1, 8:12], axis=AX))
        emf(lambda e: e.tensor_scalar_mul(SMALLF[0:1, 34:35], SMALLF[0:1, 34:35], 1.0 / NTOK_BATCH))
        emf(lambda e: e.tensor_scalar_mul(SMALLF[0:1, 35:36], SMALLF[0:1, 35:36], 1.0 / NTOK_BATCH))
        emf(lambda e: e.tensor_mul(SMALLF[0:1, 36:37], SMALLF[0:1, 34:35], SMALLF[0:1, 34:35]))
        emf(lambda e: e.tensor_sub(SMALLF[0:1, 35:36], SMALLF[0:1, 35:36], SMALLF[0:1, 36:37]))
        vv0 = cnt["sv"]
        em("scalar", lambda e: e.activation(SMALLF[0:1, 35:36], SMALLF[0:1, 35:36], AF.Ln, bias=EPSC[0:1, 0:1], scale=1.0),
           waits=[("sv", vv0)])
        em("scalar", lambda e: e.activation(SMALLF[0:1, 35:36], SMALLF[0:1, 35:36], AF.Exp, bias=0.0, scale=-0.5),
           inc="sc1")
        scv = cnt["sc1"]
        em("vector", lambda e: e.tensor_mul(SMALLF[0:1, 36:37], SMALLF[0:1, 34:35], SMALLF[0:1, 35:36]),
           waits=[("sc1", scv)], inc="sv")
        emf(lambda e: e.tensor_scalar_mul(SMALLF[0:1, 36:37], SMALLF[0:1, 36:37], -1.0))
        vv0 = cnt["sv"]
        em("tensor", lambda e: e.matmul(PC[0:128, 0:1], ONESF[0:1, :], SMALLF[0:1, 35:36],
                                        start=True, stop=True),
           waits=[("sv", vv0)])
        em("tensor", lambda e: e.matmul(PC[0:128, 1:2], ONESF[0:1, :], SMALLF[0:1, 36:37],
                                        start=True, stop=True), inc="pe")
        pv = cnt["pe"]
        em("vector", lambda e: e.tensor_copy(SMALLF[:, 32:34], PC[0:128, 0:2]),
           waits=[("pe", pv)], inc="sv")
        p2_sv = cnt["sv"]
        ckpt("p2", conv="small")

        for tt in range(16):
            slot = tt % 2
            em("sync", lambda e, tt=tt, slot=slot: e.dma_start(
                out=TOK[:, slot * 1024:(slot + 1) * 1024],
                in_=xin[tt * 128: tt * 128 + 128, :]),
               waits=[("pe", cnt["pe"])], inc="dx")
            ldv = cnt["dx"]
            for g in range(2):
                r = (tt * 2 + g) % 6
                for q in range(4):
                    kd = g * 4 + q
                    em("tensor", lambda e, slot=slot, kd=kd, r=r, q=q: e.transpose(
                        PSLOT[r][:, q * 128:(q + 1) * 128],
                        TOK[:, slot * 1024 + kd * 128: slot * 1024 + kd * 128 + 128], IDF[:]),
                       waits=[("dx", ldv), ("sv", p2_sv), ("sv", slot_sv[r])] if q == 0 else [],
                       inc="pe" if q == 3 else None)
                pv = cnt["pe"]
                kd0 = g * 4
                bigv = BIG[:].rearrange("p (a b) -> p a b", b=2048)[:, kd0:kd0 + 4,
                                                                   tt * 128:tt * 128 + 128]
                em("vector", lambda e, bigv=bigv: e.tensor_scalar(
                    MISC[:, 512:1024].rearrange("p (a b) -> p a b", b=128),
                    bigv, SMALLF[:, 32:33], SMALLF[:, 33:34], ALU.mult, ALU.add),
                   waits=[("pe", pv)])
                em("vector", lambda e, bigv=bigv, r=r: e.tensor_add(
                    bigv, MISC[:, 512:1024].rearrange("p (a b) -> p a b", b=128),
                    PSLOT[r][:].rearrange("p (a b) -> p a b", b=128)),
                   inc="sv")
                slot_sv[r] = cnt["sv"]
        p3_sv = cnt["sv"]
        ckpt("p3")

        for tc in range(4):
            for kd in range(8):
                em("vector", lambda e, kd=kd, tc=tc: e.tensor_copy(
                    CBF[:, 4096:4608], BIG[:, kd * 2048 + tc * 512: kd * 2048 + tc * 512 + 512]),
                   waits=[("sv", p3_sv), ("pe", cnt["pe"])])
                em("vector", lambda e: e.tensor_mul(CBF[:, 4608:5120], CBF[:, 4096:4608], CBF[:, 4096:4608]),
                   inc="sv")
                yv = cnt["sv"]
                em("tensor", lambda e, kd=kd: e.matmul(
                    PA[0:1, 0:512], ONESC[:], CBF[:, 4096:4608],
                    start=(kd == 0), stop=(kd == 7), skip_group_check=True),
                   waits=[("sv", yv)])
                em("tensor", lambda e, kd=kd: e.matmul(
                    PA[0:1, 512:1024], ONESC[:], CBF[:, 4608:5120],
                    start=(kd == 0), stop=(kd == 7), skip_group_check=True),
                   inc="pe")
            pv = cnt["pe"]
            em("vector", lambda e: e.tensor_scalar_mul(TOK[0:1, 0:512], PA[0:1, 0:512], 1.0 / D),
               waits=[("pe", pv)])
            em("vector", lambda e: e.tensor_scalar_mul(TOK[0:1, 512:1024], PA[0:1, 512:1024], 1.0 / D))
            em("vector", lambda e: e.tensor_mul(TOK[0:1, 1024:1536], TOK[0:1, 0:512], TOK[0:1, 0:512]))
            em("vector", lambda e: e.tensor_sub(TOK[0:1, 512:1024], TOK[0:1, 512:1024], TOK[0:1, 1024:1536]),
               inc="sv")
            vv0 = cnt["sv"]
            em("scalar", lambda e: e.activation(TOK[0:1, 512:1024], TOK[0:1, 512:1024], AF.Ln, bias=EPSC[0:1, 0:1], scale=1.0),
               waits=[("sv", vv0)])
            em("scalar", lambda e: e.activation(TOK[0:1, 1536:2048], TOK[0:1, 512:1024], AF.Exp, bias=0.0, scale=-0.5),
               inc="sc1")
            scv = cnt["sc1"]
            em("vector", lambda e: e.tensor_mul(TOK[0:1, 1024:1536], TOK[0:1, 0:512], TOK[0:1, 1536:2048]),
               waits=[("sc1", scv)])
            em("vector", lambda e: e.tensor_scalar_mul(TOK[0:1, 1024:1536], TOK[0:1, 1024:1536], -1.0),
               inc="sv")
            vv0 = cnt["sv"]
            em("tensor", lambda e: e.matmul(PB[0:128, 0:512], ONESF[0:1, :], TOK[0:1, 1536:2048],
                                            start=True, stop=True),
               waits=[("sv", vv0)])
            em("tensor", lambda e: e.matmul(PB[0:128, 512:1024], ONESF[0:1, :], TOK[0:1, 1024:1536],
                                            start=True, stop=True), inc="pe")
            pv = cnt["pe"]
            em("vector", lambda e: e.tensor_copy(MISC[:, 0:512], PB[:, 0:512]),
               waits=[("pe", pv)])
            em("vector", lambda e: e.tensor_copy(MISC[:, 512:1024], PB[:, 512:1024]), inc="sv")
            bcv = cnt["sv"]
            for kd in range(8):
                em("vector", lambda e, kd=kd, tc=tc: e.tensor_mul(
                    TOK[:, 0:512], BIG[:, kd * 2048 + tc * 512: kd * 2048 + tc * 512 + 512],
                    MISC[:, 0:512]),
                   waits=[("sv", bcv)] if kd == 0 else [])
                em("vector", lambda e, kd=kd, tc=tc: e.tensor_add(
                    A4[:, kd * 2048 + tc * 512: kd * 2048 + tc * 512 + 512],
                    TOK[:, 0:512], MISC[:, 512:1024]),
                   inc="sv" if kd == 7 else None)
        p4_sv = cnt["sv"]
        ckpt("p4", conv="a4")

        for tch in range(16):
            for kd in range(8):
                em("tensor", lambda e, tch=tch, kd=kd: e.matmul(
                    PDS[0:128, tch * 8: tch * 8 + 8],
                    A4[:, kd * 2048 + tch * 128: kd * 2048 + tch * 128 + 128],
                    RWT[:, kd * 8: kd * 8 + 8],
                    start=(kd == 0), stop=(kd == 7), skip_group_check=True),
                   waits=[("sv", p4_sv)] if (tch == 0 and kd == 0) else [],
                   inc="pe" if (tch == 15 and kd == 7) else None)
        pv = cnt["pe"]
        em("vector", lambda e: e.tensor_copy(CW[:], PDS[0:128, 0:128]),
           waits=[("pe", pv)], inc="sv")
        cw3 = CW[:].rearrange("p (t e) -> p t e", e=8)
        emf(lambda e: e.reduce_max(TOK[:, 0:16], cw3, axis=AX))
        emf(lambda e: e.tensor_tensor(
            cw3, cw3, TOK[:, 0:16].rearrange("p (t o) -> p t o", o=1).broadcast_to([128, 16, 8]),
            ALU.subtract))
        vv0 = cnt["sv"]
        em("scalar", lambda e: e.activation(CW[:], CW[:], AF.Exp, bias=0.0, scale=1.0),
           waits=[("sv", vv0)], inc="sc1")
        scv = cnt["sc1"]
        em("vector", lambda e: e.reduce_sum(TOK[:, 16:32], cw3, axis=AX),
           waits=[("sc1", scv)], inc="sv")
        emf(lambda e: e.reciprocal(TOK[:, 32:48], TOK[:, 16:32]))
        emf(lambda e: e.tensor_tensor(
            cw3, cw3, TOK[:, 32:48].rearrange("p (t o) -> p t o", o=1).broadcast_to([128, 16, 8]),
            ALU.mult))
        def cwslice(e8):
            return CW[:].rearrange("p (t e) -> p t e", e=8)[:, :, e8:e8 + 1]
        for ei in range(8):
            rsl = TOK[:, 512 + ei * 16: 512 + ei * 16 + 16]
            firstj = True
            for j in range(8):
                if j == ei:
                    continue
                op = ALU.is_ge if j < ei else ALU.is_gt
                if firstj:
                    emf(lambda e, rsl=rsl, j=j, ei=ei, op=op: e.tensor_tensor(
                        rsl.rearrange("p (t o) -> p t o", o=1), cwslice(j), cwslice(ei), op))
                    firstj = False
                else:
                    emf(lambda e, j=j, ei=ei, op=op: e.tensor_tensor(
                        TOK[:, 256:272].rearrange("p (t o) -> p t o", o=1), cwslice(j), cwslice(ei), op))
                    emf(lambda e, rsl=rsl: e.tensor_add(rsl, rsl, TOK[:, 256:272]))
        emf(lambda e: e.memset(TOK[:, 272:288], 0.0))
        for ei in range(8):
            rsl = TOK[:, 512 + ei * 16: 512 + ei * 16 + 16]
            emf(lambda e, rsl=rsl: e.tensor_single_scalar(rsl, rsl, 2.0, ALU.is_lt))
            emf(lambda e, rsl=rsl, ei=ei: e.tensor_tensor(
                cwslice(ei), cwslice(ei), rsl.rearrange("p (t o) -> p t o", o=1), ALU.mult))
            emf(lambda e, ei=ei: e.tensor_tensor(
                TOK[:, 272:288].rearrange("p (t o) -> p t o", o=1),
                TOK[:, 272:288].rearrange("p (t o) -> p t o", o=1), cwslice(ei), ALU.add))
        emf(lambda e: e.tensor_scalar_add(TOK[:, 272:288], TOK[:, 272:288], 1e-9))
        emf(lambda e: e.reciprocal(TOK[:, 288:304], TOK[:, 272:288]))
        for ei in range(8):
            emf(lambda e, ei=ei: e.tensor_tensor(
                cwslice(ei), cwslice(ei),
                TOK[:, 288:304].rearrange("p (t o) -> p t o", o=1), ALU.mult))
        p5_sv = cnt["sv"]
        ckpt("p5", conv="small")

        ffn_war_sv = p5_sv
        ffn_pe_base = cnt["pe"]
        for e_i in range(NEXP):
            for TC in range(2):
                t0 = TC * TCF
                for w in range(2):
                    dstc = 0 if w == 0 else 1024
                    for i4 in range(4):
                        col = (TC * 8 + w * 4 + i4) * 8 + e_i
                        em("tensor", lambda e, col=col, i4=i4: e.transpose(
                            PDS[0:1, i4 * 128: i4 * 128 + 128], CW[:, col:col + 1], IDF[:]),
                           waits=([("sv", cnt["sv"])] if i4 == 0 else []),
                           inc="pe" if i4 == 3 else None)
                    pv = cnt["pe"]
                    em("vector", lambda e: e.tensor_copy(TOK[0:1, 0:512], PDS[0:1, 0:512]),
                       waits=[("pe", pv)], inc="sv")
                    vv0 = cnt["sv"]
                    em("tensor", lambda e: e.matmul(PDS[0:128, 0:512], ONESF[0:1, :],
                                                    TOK[0:1, 0:512], start=True, stop=True),
                       waits=[("sv", vv0)], inc="pe")
                    pv = cnt["pe"]
                    em("vector", lambda e, dstc=dstc: e.tensor_copy(MISC[:, dstc:dstc + 512], PDS[:, 0:512]),
                       waits=[("pe", pv)], inc="sv")
                cw_sv = cnt["sv"]
                ckpt("cwb_%d_%d" % (e_i, TC))
                gelu_base = cnt["sc1"]
                hchain_pe = {}
                for j in range(NJ):
                    dslot = (j % 4) * 1024
                    dwait = hchain_pe.get(j - 4, ffn_pe_base)
                    em("gpsimd", lambda e, e_i=e_i, j=j, dslot=dslot: e.dma_start(
                        out=DBF[:, dslot:dslot + 1024].rearrange("p (k c) -> p k c", c=128),
                        in_=w1[e_i, :, j * 128:(j + 1) * 128].rearrange("(k p) c -> p k c", p=128)),
                       waits=[("pe", dwait)],
                       inc="dg")
                    wld = cnt["dg"]
                    ps = PA if j % 2 == 0 else PB
                    pswait = gelu_base + j - 1 if j >= 2 else 0
                    for ts2 in range(2):
                        for kd in range(8):
                            wl = []
                            if ts2 == 0 and kd == 0:
                                wl = [("dg", wld), ("sc1", pswait), ("sv", cw_sv)]
                            em("tensor", lambda e, j=j, ts2=ts2, kd=kd, ps=ps, dslot=dslot, t0=t0: e.matmul(
                                ps[:, ts2 * 512: ts2 * 512 + 512],
                                DBF[:, dslot + kd * 128: dslot + kd * 128 + 128],
                                A4[:, kd * 2048 + t0 + ts2 * 512: kd * 2048 + t0 + ts2 * 512 + 512],
                                start=(kd == 0), stop=(kd == 7), skip_group_check=True),
                               waits=wl,
                               inc="pe" if (ts2 == 1 and kd == 7) else None)
                    pv = cnt["pe"]
                    hchain_pe[j] = pv
                    em("scalar", lambda e, j=j, ps=ps: e.activation(
                        HB[:, j * 1024:(j + 1) * 1024], ps[:, 0:1024], AF.Gelu, bias=0.0, scale=1.0),
                       waits=[("pe", pv)], inc="sc1")
                gelu_all = cnt["sc1"]
                ffn_pe_base = cnt["pe"]
                ckpt("h_%d_%d" % (e_i, TC), conv="hb")
                y_evac_sv = {}
                y_chain_sv = {}
                for d_i in range(8):
                    cslot = (d_i % 2) * 4096
                    war = y_evac_sv.get(d_i - 2, ffn_war_sv)
                    for quarter in range(4):
                        em("gpsimd", lambda e, e_i=e_i, d_i=d_i, q4=quarter, cslot=cslot: e.dma_start(
                            out=CBF[:, cslot + q4 * 1024: cslot + q4 * 1024 + 1024]
                            .rearrange("p (j c) -> p j c", c=128),
                            in_=w2[e_i, q4 * 1024:(q4 + 1) * 1024, d_i * 128:(d_i + 1) * 128]
                            .rearrange("(j p) c -> p j c", p=128)),
                           waits=[("sv", war)] if quarter == 0 else [],
                           inc="dg")
                    wld = cnt["dg"]
                    for tsub in range(2):
                        ci = d_i * 2 + tsub
                        psy = PAV[:, 0:512] if ci % 2 == 0 else PAV[:, 512:1024]
                        cwar = y_chain_sv.get(ci - 2, war)
                        for j in range(NJ):
                            wl = []
                            if j == 0:
                                wl = [("dg", wld), ("sc1", gelu_all), ("sv", max(war, cwar))]
                            em("tensor", lambda e, d_i=d_i, tsub=tsub, j=j, psy=psy, cslot=cslot: e.matmul(
                                psy, CBF[:, cslot + j * 128: cslot + j * 128 + 128],
                                HB[:, j * 1024 + tsub * 512: j * 1024 + tsub * 512 + 512],
                                start=(j == 0), stop=(j == NJ - 1), skip_group_check=True),
                               waits=wl,
                               inc="pe" if j == NJ - 1 else None)
                        pv = cnt["pe"]
                        wsl = 0 if tsub == 0 else 1024
                        em("vector", lambda e, psy=psy, wsl=wsl: e.tensor_mul(
                            MISC[:, 512:1024], psy, MISC[:, wsl:wsl + 512]),
                           waits=[("pe", pv)])
                        em("vector", lambda e, d_i=d_i, t0=t0, tsub=tsub: e.tensor_add(
                            BIG[:, d_i * 2048 + t0 + tsub * 512: d_i * 2048 + t0 + tsub * 512 + 512],
                            BIG[:, d_i * 2048 + t0 + tsub * 512: d_i * 2048 + t0 + tsub * 512 + 512],
                            MISC[:, 512:1024]),
                           inc="sv")
                        y_chain_sv[ci] = cnt["sv"]
                    y_evac_sv[d_i] = cnt["sv"]
                    ckpt("yd_%d_%d_%d" % (e_i, TC, d_i))
                ffn_war_sv = cnt["sv"]
                ckpt("ffn_%d_%d" % (e_i, TC))

        ckpt("p6")
        ffn_sv = cnt["sv"]
        for tt in range(16):
            slot = tt % 2
            for g in range(2):
                r = (tt * 2 + g) % 6
                for q in range(4):
                    kd = g * 4 + q
                    em("tensor", lambda e, kd=kd, tt=tt, r=r, q=q: e.transpose(
                        PSLOT[r][:, q * 128:(q + 1) * 128],
                        BIG[:, kd * 2048 + tt * 128: kd * 2048 + tt * 128 + 128], IDF[:]),
                       waits=([("sv", ffn_sv), ("dx", cnt["dx"]), ("sv", slot_sv[r])] if q == 0 else []),
                       inc="pe" if q == 3 else None)
                pv = cnt["pe"]
                em("vector", lambda e, slot=slot, g=g, r=r: e.tensor_copy(
                    TOK[:, slot * 1024 + g * 512: slot * 1024 + g * 512 + 512], PSLOT[r][:]),
                   waits=[("pe", pv)], inc="sv")
                slot_sv[r] = cnt["sv"]
            vv0 = cnt["sv"]
            em("sync", lambda e, tt=tt, slot=slot: e.dma_start(
                out=out[tt * 128: tt * 128 + 128, :],
                in_=TOK[:, slot * 1024: slot * 1024 + 1024]),
               waits=[("sv", vv0)], inc="dx")

        final_dx = cnt["dx"]
        final_counts = dict(cnt)

        def fin(eng, v=final_dx):
            eng.wait_ge(sems["dx"], v)
        PLAN["sync"].append(fin)

        @block.sync
        def _(eng):
            for f in PLAN["sync"]:
                f(eng)

        @block.gpsimd
        def _(eng):
            for f in PLAN["gpsimd"]:
                f(eng)

        @block.tensor
        def _(eng):
            for f in PLAN["tensor"]:
                f(eng)

        @block.vector
        def _(eng):
            for f in PLAN["vector"]:
                f(eng)

        @block.scalar
        def _(eng):
            for f in PLAN["scalar"]:
                f(eng)

    return nc


_NC_CACHE = {}


def kernel(L0_attn_x, L0_attn_Wq, L0_attn_Wk, L0_attn_Wv,
           L0_ffn_router_w, L0_ffn_w1, L0_ffn_w2):
    x = np.ascontiguousarray(np.asarray(L0_attn_x, dtype=np.float32)).reshape(16384, 1024)
    wq = np.ascontiguousarray(np.asarray(L0_attn_Wq, dtype=np.float32))
    wk = np.ascontiguousarray(np.asarray(L0_attn_Wk, dtype=np.float32))
    wv = np.ascontiguousarray(np.asarray(L0_attn_Wv, dtype=np.float32))
    rw = np.ascontiguousarray(np.asarray(L0_ffn_router_w, dtype=np.float32))
    w1 = np.ascontiguousarray(np.asarray(L0_ffn_w1, dtype=np.float32))
    w2 = np.ascontiguousarray(np.asarray(L0_ffn_w2, dtype=np.float32))

    if "nc" not in _NC_CACHE:
        _NC_CACHE["nc"] = build_program()
    nc = _NC_CACHE["nc"]

    in_maps = []
    for c in range(NCORES):
        b = c // 4
        ms = np.zeros((1, 12), dtype=np.float32)
        if b == 0:
            ms[0, 0] = 1.0
            ms[0, 1] = 1.0
            ms[0, 4] = 1.0
            ms[0, 9] = 1.0
        else:
            ms[0, 2] = 1.0
            ms[0, 3] = 1.0
            ms[0, 6] = 1.0
            ms[0, 11] = 1.0
        in_maps.append({
            "xin": x[c * TCORE:(c + 1) * TCORE],
            "wq": wq, "wk": wk, "wv": wv, "rw": rw,
            "w1": w1, "w2": w2, "msel": ms,
        })
    res = run_bass_kernel_spmd(nc, in_maps, core_ids=list(range(NCORES)), trace=TRACE)
    if TRACE and getattr(res, "exec_time_ns", None):
        print("KERNEL exec_time_ns:", res.exec_time_ns)
    full = np.concatenate([np.asarray(res.results[c]["out"]) for c in range(NCORES)], axis=0)
    return full.reshape(2, 8192, 1024).astype(np.float32)
